# version 26
# baseline (speedup 1.0000x reference)
import contextlib
import numpy as np
import ml_dtypes

import concourse.bass as bass
import concourse.mybir as mybir
import bass_rust
from concourse import library_config
from concourse.bass_utils import run_bass_kernel_spmd

F32 = mybir.dt.float32
BF16 = mybir.dt.bfloat16
AF = mybir.ActivationFunctionType
ALU = mybir.AluOpType
AX = mybir.AxisListType

C = 1024
H = 16
MAGIC = float(np.float32(3 * 2 ** 22))


class Ctr:
    def __init__(self, h):
        self.h = h
        self.v = 0


class Prog:
    ENGINES = ("sync", "gpsimd", "vector", "scalar", "tensor")

    def __init__(self):
        self.q = {e: [] for e in self.ENGINES}

    def run(self, block):
        for eng in self.ENGINES:
            fns = self.q[eng]
            getattr(block, eng)(lambda e, fns=fns: [f(e) for f in fns])


STOP_PHASE = None


def build(T=4096, taps=()):
    nc = bass.Bass("TRN2", target_bir_lowering=False)
    NCH = T // 128
    NTB = T // 512
    NST = T // 128

    x_d = nc.dram_tensor("x", [T, C], F32, kind="ExternalInput").ap()
    wd = {}
    for nm in ("Wsq", "Wsk", "Wsv", "Wq", "Wk", "Wv", "Wo", "fk"):
        wd[nm] = nc.dram_tensor(nm, [C, C], F32, kind="ExternalInput").ap()
    for nm in ("bsq", "bsk", "bsv", "bq", "bk", "bv", "bo", "fbias"):
        wd[nm] = nc.dram_tensor(nm, [1, C], F32, kind="ExternalInput").ap()
    wd["Wg"] = nc.dram_tensor("Wg", [H, C], F32, kind="ExternalInput").ap()
    wd["bg"] = nc.dram_tensor("bg", [1, H], F32, kind="ExternalInput").ap()
    wd["temp"] = nc.dram_tensor("temp", [1, H], F32, kind="ExternalInput").ap()
    wd["fa"] = nc.dram_tensor("fa", [C, 16], F32, kind="ExternalInput").ap()
    wd["fb"] = nc.dram_tensor("fb", [16, C], F32, kind="ExternalInput").ap()
    ident_d = nc.dram_tensor("ident", [128, 128], BF16, kind="ExternalInput").ap()
    btri_d = nc.dram_tensor("btri", [128, 128], F32, kind="ExternalInput").ap()
    bones_d = nc.dram_tensor("bones", [128, 256], F32, kind="ExternalInput").ap()
    mask2_d = nc.dram_tensor("mask2", [128, 64], BF16, kind="ExternalInput").ap()
    ind_d = nc.dram_tensor("ind", [128, 256], F32, kind="ExternalInput").ap()
    idf_d = nc.dram_tensor("identf", [128, 128], F32, kind="ExternalInput").ap()
    out_d = nc.dram_tensor("out", [T, C], F32, kind="ExternalOutput").ap()

    vext_s = nc.dram_tensor("vext_s", [128, T * 8], BF16, kind="Internal").ap()
    qt_s = nc.dram_tensor("qt_s", [128, T * 8], BF16, kind="Internal").ap()
    attn_s = nc.dram_tensor("attn_s", [T, C], BF16, kind="Internal").ap()

    tshape = {"xn": [T, C], "sq": [T, C], "sk2": [T, C], "sv": [T, C], "v": [T, C],
              "sp": [T, H], "hseq": [T, C], "kc": [T, C], "qn": [T, C],
              "attn": [T, C], "x1": [T, C], "p3": [128, 2048], "aot": [128, 4096],
              "rv": [128, 1024], "vx": [128, T * 8], "ovp": [128, 2048], "wot": [128, 8192], "wop": [128, 2048]}
    tap_d = {t_: nc.dram_tensor("tap_" + t_, tshape[t_], BF16 if t_ in ("attn",) else F32,
                                kind="ExternalOutput").ap()
             for t_ in taps}

    p = Prog()
    _en = [True]

    def on(eng, fn):
        if _en[0]:
            p.q[eng].append(fn)

    ctx = contextlib.ExitStack()
    with ctx:
        AR = 98304
        arena = ctx.enter_context(nc.sbuf_tensor("arena", [128, AR], BF16))

        def B(off, n, p0=0, p1=128):
            return arena[p0:p1, off:off + n]

        def Fv(off, n, p0=0, p1=128):
            assert off % 2 == 0
            return arena[p0:p1, off:off + 2 * n].bitcast(F32)

        def cst(v, p0=0, p1=128):
            return Fv(o_cst + 2 * CSTV[v], 1, p0, p1)

        class Alloc:
            def __init__(self, base):
                self.o = base

            def __call__(self, n):
                self.o = (self.o + 1) // 2 * 2
                r = self.o
                self.o += n
                assert self.o <= AR, self.o
                return r

        g_al = Alloc(0)
        o_big = g_al(T * 8)
        KTR = max(T * 8, 32768)
        o_ktr = g_al(KTR)
        o_w2 = [g_al(8192), g_al(8192)]
        o_id = g_al(128)
        o_cb = g_al(32)
        o_onec = g_al(2)
        o_oner = g_al(128)
        o_kmax = g_al(16)
        o_cst = g_al(32)
        o_spc = g_al(4)
        o_mask = g_al(64)
        o_idf = g_al(256)
        o_onef = g_al(256)
        OVB = g_al.o
        CSTV = {1e-5: 0, 1.0: 1, 1e-19: 2, -0.5: 3, -1.0: 4}

        o_kt = o_ktr
        k1 = Alloc(o_ktr)
        o_wsv = k1(8192)
        o_wv = k1(8192)
        o_xst = k1(2 * 2048)
        o_wst = k1(2 * 2048)
        o_lnsq = k1(2048)
        o_vst = k1(2 * 1024)
        o_m = k1(2 * 512)
        o_pp = k1(1024)
        o_xnn = k1(1024)
        o_xnT = k1(1024)
        assert k1.o <= o_ktr + KTR, k1.o
        o_w = [o_w2[0], o_w2[1], o_wsv, o_wv]

        a1 = Alloc(OVB)
        o_pl = [a1(1024) for _ in range(4)]
        o_plg = a1(16)
        o_btri = a1(256)
        o_bones = a1(512)
        o_wg = a1(128)
        o_small = a1(256)
        o_sqn = a1(1024)
        o_skn = a1(1024)
        o_sk2 = a1(1024)
        o_svn = a1(1024)
        o_sqT = a1(2 * 1024)
        o_sk2T = a1(1024)
        o_hn = a1(1024)
        o_hn2 = a1(1024)
        o_xnT2 = o_hn2
        o_sp = a1(32)
        o_eml = a1(64)
        o_el = a1(64)
        o_ea = a1(64)
        o_ep = a1(64)
        o_st1 = a1(64)
        o_st2 = a1(128)
        o_rstd = a1(128)
        o_brow = o_xst
        p1_end = a1.o

        a3 = Alloc(OVB)
        o_p3 = a3(2 * 2048)
        o_aot = a3(4096)
        o_rv = a3(2048)
        o_bcv = a3(1024)
        o_att = a3(2 * 1024)
        o_ind = a3(512)
        o_pl3 = a3(1024)
        o_qtb = o_w2[1]
        p3_end = a3.o

        a4 = Alloc(o_ktr)
        o_x4a = a4(2 * 2048)
        o_x4b = a4(2 * 1024)
        o_fst = a4(2 * 2048)
        o_wq4 = a4(2048)
        o_xn2 = a4(2 * 1024)
        o_x2T = a4(1024)
        o_aT = a4(1024)
        o_fbB = a4(1024)
        o_ou = a4(2 * 2048)
        o_s4 = a4(64)
        o_lnsq4 = a4(2048)
        o_pl4 = a4(1024)
        assert a4.o <= o_ktr + KTR, a4.o
        assert max(p1_end, p3_end) <= AR, (p1_end, p3_end, AR)

        psA = ctx.enter_context(nc.psum_tensor("psA", [128, 2048], F32))
        psB = ctx.enter_context(nc.psum_tensor("psB", [128, 2048], F32))
        psA_bf = psA.ap().bitcast(BF16)

        def A(i, p0=0, p1=128, c0=0, c1=512):
            return psA[p0:p1, i * 512 + c0:i * 512 + c1]

        def Bk(i, p0=0, p1=128, c0=0, c1=512):
            return psB[p0:p1, i * 512 + c0:i * 512 + c1]

        def A2bf(c0, c1, p0=0, p1=128):
            return psA_bf[p0:p1, 2048 + c0:2048 + c1]

        dS = Ctr(ctx.enter_context(nc.semaphore("dS")))
        dG = Ctr(ctx.enter_context(nc.semaphore("dG")))
        tE = Ctr(ctx.enter_context(nc.semaphore("tE")))
        vE = Ctr(ctx.enter_context(nc.semaphore("vE")))
        sE = Ctr(ctx.enter_context(nc.semaphore("sE")))
        gC = Ctr(ctx.enter_context(nc.semaphore("gC")))
        dX = Ctr(ctx.enter_context(nc.semaphore("dX")))
        dT = Ctr(ctx.enter_context(nc.semaphore("dT")))
        dF = Ctr(ctx.enter_context(nc.semaphore("dF")))

        def w(eng, ctr, v=None):
            v = ctr.v if v is None else v
            if v > 0:
                on(eng, lambda e, h=ctr.h, v=v: e.wait_ge(h, v))

        def stamp(eng, ctr, fn):
            ctr.v += 1
            on(eng, lambda e, fn=fn, h=ctr.h: fn(e).then_inc(h, 1))
            return ctr.v

        def dma(eng, ctr, out, in_):
            ctr.v += 16
            on(eng, lambda e, o=out, i=in_, h=ctr.h: e.dma_start(out=o, in_=i).then_inc(h, 16))
            return ctr.v

        def vsync():
            w("vector", vE)

        def mm(out, lhsT, rhs, start, stop, tile=None):
            on("tensor", lambda e, o=out, l=lhsT, r=rhs, s=start, st=stop, tp=tile:
               e.matmul(o, l, r, start=s, stop=st, tile_position=tp, skip_group_check=True))

        def tdrain():
            return stamp("tensor", tE, lambda e: e.drain(fusable=False))

        def mml(out, lhsT, rhs, start, stop, tile=None):
            tE.v += 1
            on("tensor", lambda e, o=out, l=lhsT, r=rhs, st_=start, sp=stop, tp=tile, h=tE.h:
               e.matmul(o, l, r, start=st_, stop=sp, tile_position=tp, skip_group_check=True).then_inc(h, 1))
            return tE.v

        def bcast_h(off_f32, g=H, w_=64, p0=0, p1=128):
            return Fv(off_f32, g, p0, p1).unsqueeze(2).broadcast_to([p1 - p0, g, w_])

        def g3(ap, g=H):
            return ap.rearrange("p (a b) -> p a b", a=g)

        def transpose_group(srcs, dst, tw=128):
            n = len(srcs)
            for i, s in enumerate(srcs):
                if i < n - 1:
                    on("tensor", lambda e, s=s, o=A2bf(i * tw, (i + 1) * tw), f=(i == 0),
                       ид=arena[:, o_id:o_id + 128]: e.matmul(o, s, ид, is_transpose=True, start=f, stop=False, skip_group_check=True))
                else:
                    tE.v += 1
                    on("tensor", lambda e, s=s, o=A2bf(i * tw, (i + 1) * tw), f=(i == 0), h=tE.h:
                       e.matmul(o, s, arena[:, o_id:o_id + 128], is_transpose=True, start=f, stop=True,
                                skip_group_check=True).then_inc(h, 1))
            tv = tE.v
            w("vector", tE, tv)
            vE.v += 1
            src = A2bf(0, n * tw)
            if dst.ndim == 3:
                src = src.rearrange("p (a b) -> p a b", b=tw)
            on("vector", lambda e, d=dst, src=src, h=vE.h:
               e.tensor_copy(d, src).then_inc(h, 1))
            ev = vE.v
            w("tensor", vE, ev)
            return tv, ev

        def ln_chunk(xo, outo, so, sqo):
            se = ln_stats(xo, so, sqo)
            return ln_norm(xo, outo, so, se)

        def tap_store(name, ap, row0, n=128):
            if name in tap_d:
                w("gpsimd", vE)
                w("gpsimd", sE)
                w("gpsimd", dT)
                dma("gpsimd", dT, tap_d[name][row0:row0 + n, :], ap)

        def tap_sync(eng):
            if tap_d:
                w(eng, dT)

        def build_plane(bias_ap, plane_off, width, brow=None):
            brow = o_brow if brow is None else brow
            tap_sync("sync")
            tap_sync("vector")
            w("sync", vE)
            dv = dma("sync", dS, Fv(brow, width, 0, 1), bias_ap)
            w("vector", dS, dv)
            cv = stamp("vector", vE, lambda e, w_=width, br=brow:
                       e.tensor_copy(B(br + 2048, w_, 0, 1), Fv(br, w_, 0, 1)))
            w("tensor", vE, cv)
            nh = (width + 511) // 512
            for i in range(nh):
                wdt = min(512, width - i * 512)
                mm(psA[:, 1024 + i * 512:1024 + i * 512 + wdt],
                   arena[0:1, o_oner:o_oner + 128],
                   B(brow + 2048 + i * 512, wdt, 0, 1), True, True)
            tv = tdrain()
            w("vector", tE, tv)
            ev = stamp("vector", vE, lambda e, po=plane_off, w_=width:
                       e.tensor_copy(B(po, w_), psA[:, 1024:1024 + w_]))
            w("tensor", vE, ev)
            return ev

        def prep_weightT(w_ap, slot_off, stg=None, stg2=None):
            stg = o_wst if stg is None else stg
            stg2 = o_sqn if stg2 is None else stg2
            tap_sync("vector")
            tap_sync("sync")
            for kc in range(8):
                bsel = kc % 2
                w("sync", vE)
                dv = dma("sync", dS, Fv(stg + bsel * 2048, 1024), w_ap[kc * 128:(kc + 1) * 128, :])
                w("vector", dS, dv)
                cv = stamp("vector", vE, lambda e, b=bsel, g_=stg, g2=stg2:
                           e.tensor_copy(B(g2, 1024), Fv(g_ + b * 2048, 1024)))
                w("tensor", vE, cv)
                dst = B(slot_off, 8192).rearrange("p (a b) -> p a b", b=1024)[:, :, kc * 128:(kc + 1) * 128]
                transpose_group([B(stg2 + i * 128, 128) for i in range(8)], dst)

        def ln_stats(xo, so, sqo):
            stamp("vector", vE, lambda e, x=Fv(xo, 1024), o=Fv(so, 1): e.reduce_sum(o, x, axis=AX.X))
            stamp("vector", vE, lambda e, x=Fv(xo, 1024), o=Fv(sqo, 1024): e.tensor_tensor(o, x, x, op=ALU.mult))
            vsync()
            stamp("vector", vE, lambda e, x=Fv(sqo, 1024), o=Fv(so + 2, 1): e.reduce_sum(o, x, axis=AX.X))
            stamp("vector", vE, lambda e, i=Fv(so, 1), o=Fv(so + 4, 1): e.tensor_scalar(o, i, 1.0 / C, None, op0=ALU.mult))
            vsync()
            stamp("vector", vE, lambda e, i=Fv(so + 2, 1), o=Fv(so + 6, 1): e.tensor_scalar(o, i, 1.0 / C, None, op0=ALU.mult))
            stamp("vector", vE, lambda e, i=Fv(so + 4, 1), o=Fv(so + 8, 1): e.tensor_tensor(o, i, i, op=ALU.mult))
            vsync()
            sv_ = stamp("vector", vE, lambda e, a=Fv(so + 6, 1), b=Fv(so + 8, 1), o=Fv(so + 10, 1):
                        e.tensor_tensor(o, a, b, op=ALU.subtract))
            w("scalar", vE, sv_)
            sl = stamp("scalar", sE, lambda e, i=Fv(so + 10, 1), o=Fv(so + 12, 1):
                       e.activation(o, i, AF.Ln, bias=cst(1e-5)))
            w("scalar", sE, sl)
            se = stamp("scalar", sE, lambda e, i=Fv(so + 12, 1), o=Fv(so + 14, 1):
                       e.activation(o, i, AF.Exp, scale=cst(-0.5)))
            return se

        def ln_norm(xo, outo, so, se):
            w("vector", sE, se)
            vsync()
            rv = stamp("vector", vE, lambda e, x=Fv(xo, 1024), o=B(outo, 1024), m=Fv(so + 4, 1), r=Fv(so + 14, 1):
                       e.tensor_scalar(o, x, m, r, op0=ALU.subtract, op1=ALU.mult))
            return rv

        dma("sync", dS, arena[:, o_id:o_id + 128], ident_d)
        dma("sync", dS, Fv(o_btri, 128), btri_d)
        dma("sync", dS, Fv(o_bones, 256), bones_d)
        dma("sync", dS, arena[:, o_mask:o_mask + 64], mask2_d)
        dma("sync", dS, Fv(o_idf, 128), idf_d)
        dv_temp = dma("sync", dS, Fv(o_small, H, 0, 1), wd["temp"])
        on("vector", lambda e: e.memset(arena[0:1, o_oner:o_oner + 128], 1.0))
        on("vector", lambda e: e.memset(Fv(o_onef, 128, 0, 1), 1.0))
        on("vector", lambda e: e.memset(arena[:, o_onec:o_onec + 1], 1.0))
        on("vector", lambda e: e.memset(B(o_m, 1024), 0.0))
        for _v, _i in CSTV.items():
            on("vector", lambda e, o=Fv(o_cst + 2 * _i, 1), v=float(_v): e.memset(o, v))
        mz = stamp("vector", vE, lambda e: e.memset(Fv(o_kmax, 8), 0.0))
        w("tensor", dS)
        w("scalar", dS, dv_temp)
        w("scalar", vE, mz)
        pe1 = stamp("scalar", sE, lambda e: e.activation(Fv(o_small + 32, H, 0, 1), Fv(o_small, H, 0, 1), AF.Exp))
        w("scalar", sE, pe1)
        sp_t = stamp("scalar", sE, lambda e: e.activation(Fv(o_small + 64, H, 0, 1), Fv(o_small + 32, H, 0, 1),
                                                          AF.Ln, bias=cst(1.0, 0, 1)))
        w("vector", sE, sp_t)
        w("vector", vE, mz)
        cb_c = stamp("vector", vE, lambda e: e.tensor_copy(B(o_brow + 2048, H, 0, 1), Fv(o_small + 64, H, 0, 1)))
        w("tensor", vE, cb_c)
        mm(psA[:, 1536:1536 + H], arena[0:1, o_oner:o_oner + 128], B(o_brow + 2048, H, 0, 1), True, True)
        tv = tdrain()
        w("vector", tE, tv)
        cb_e = stamp("vector", vE, lambda e: e.tensor_copy(Fv(o_cb, H), psA[:, 1536:1536 + H]))
        w("tensor", vE, cb_e)

        for wnm, slot in (("Wsq", 0), ("Wsk", 1), ("Wsv", 2), ("Wv", 3)):
            prep_weightT(wd[wnm], o_w[slot])
        w("sync", vE)
        dvg = dma("sync", dS, Fv(o_wst, 1024, 0, 16), wd["Wg"])
        w("vector", dS, dvg)
        cvg = stamp("vector", vE, lambda e: e.tensor_copy(B(o_sqn, 1024, 0, 16), Fv(o_wst, 1024, 0, 16)))
        w("tensor", vE, cvg)
        for i in range(8):
            on("tensor", lambda e, s=B(o_sqn + i * 128, 128, 0, 16), o=A2bf(i * 16, (i + 1) * 16),
               f=(i == 0), l=(i == 7), ид=arena[0:16, o_id:o_id + 16]:
               e.matmul(o, s, ид, is_transpose=True, start=f, stop=l, skip_group_check=True))
        tv = tdrain()
        w("vector", tE, tv)
        wgv = stamp("vector", vE, lambda e: e.tensor_copy(B(o_wg, 128), A2bf(0, 128)))
        w("tensor", vE, wgv)
        for bnm, po, wd_ in (("bsq", o_pl[0], C), ("bsk", o_pl[1], C), ("bsv", o_pl[2], C),
                             ("bv", o_pl[3], C), ("bg", o_plg, H)):
            build_plane(wd[bnm], po, wd_)

        if STOP_PHASE is not None and STOP_PHASE < 1:
            _en[0] = False
        w("vector", vE)
        w("sync", vE)
        w("sync", sE)
        x_dma = [0] * NCH
        for c in range(min(2, NCH)):
            x_dma[c] = dma("sync", dX, Fv(o_xst + (c % 2) * 2048, 1024), x_d[c * 128:(c + 1) * 128, :])
        fk_red = [0] * 8
        fk_dma = [0] * 8
        vdma_prev = [0, 0]
        m_par = 0
        hv_prev = 0
        sev_prev = 0
        ln_done = [0] * NCH
        skv = 0

        def xnt_of(par):
            return o_xnT if par == 0 else o_xnT2

        def round1(wslot, dsto, plane, xnt_o, also_g=False):
            tv = 0
            for kc in range(8):
                for hf in range(2):
                    last = (kc == 7 and hf == 1 and not also_g)
                    f = mml if last else mm
                    tv = f(A(hf), B(xnt_o + kc * 128, 128), B(wslot + kc * 1024 + hf * 512, 512),
                           kc == 0, kc == 7) or tv
                if also_g:
                    f = mml if kc == 7 else mm
                    tv = f(psB[:, 1536:1536 + H], B(xnt_o + kc * 128, 128), B(o_wg + kc * 16, 16),
                           kc == 0, kc == 7) or tv
            w("vector", tE, tv)
            ev = stamp("vector", vE, lambda e, d=B(dsto, 1024), pl=B(plane, 1024):
                       e.tensor_tensor(d, psA[:, 0:1024], pl, op=ALU.add))
            w("tensor", vE, ev)
            return ev

        xnt_tv = [0]
        ln_se = [0]

        def frontA(c):
            bsel = c % 2
            tap_sync("vector")
            tap_sync("scalar")
            w("vector", dX)
            ln_se[0] = ln_stats(o_xst + bsel * 2048, o_st1, o_lnsq)

            if c < 8:
                if c >= 1:
                    w("sync", vE, fk_red[c - 1])
                else:
                    w("sync", vE)
                fk_dma[c] = dma("sync", dF, Fv(o_wst + bsel * 2048, 1024), wd["fk"][c * 128:(c + 1) * 128, :])
                w("vector", dF)
                w("vector", vE)
                stamp("vector", vE, lambda e, i=Fv(o_wst + bsel * 2048, 1024), o=Fv(o_st2 + 96, 1):
                      e.tensor_reduce(o, i, axis=AX.X, op=ALU.max, apply_absolute_value=True))
                vsync()
                fk_red[c] = stamp("vector", vE, lambda e: e.tensor_tensor(
                    Fv(o_kmax, 1), Fv(o_kmax, 1), Fv(o_st2 + 96, 1), op=ALU.max))

        def frontB(c):
            nonlocal sev_prev
            bsel = c % 2
            par = c % 2
            w("vector", tE, xnt_tv[0])
            lnv = ln_norm(o_xst + bsel * 2048, o_xnn, o_st1, ln_se[0])
            ln_done[c] = lnv
            tap_store("xn", B(o_xnn, 1024), c * 128)
            if c + 2 < NCH:
                w("sync", vE, lnv)
                x_dma[c + 2] = dma("sync", dX, Fv(o_xst + (c % 2) * 2048, 1024),
                                   x_d[(c + 2) * 128:(c + 3) * 128, :])
            w("tensor", vE, lnv)
            tg, _ = transpose_group([B(o_xnn + i * 128, 128) for i in range(8)], B(xnt_of(par), 1024))
            xnt_tv[0] = tg
            w("tensor", sE, sev_prev)
            round1(o_w[0], o_sqn, o_pl[0], xnt_of(par), also_g=True)
            tap_store("sq", B(o_sqn, 1024), c * 128)
            gv = stamp("vector", vE, lambda e: e.tensor_tensor(
                Fv(o_st2, H), psB[:, 1536:1536 + H], B(o_plg, H), op=ALU.add))
            w("scalar", vE, gv)
            ge1 = stamp("scalar", sE, lambda e: e.activation(Fv(o_st2 + 32, H), Fv(o_st2, H), AF.Exp))
            w("scalar", sE, ge1)
            spv = stamp("scalar", sE, lambda e: e.activation(Fv(o_sp, H), Fv(o_st2 + 32, H), AF.Ln, bias=cst(1.0)))
            tap_store("sp", Fv(o_sp, H), c * 128)

            w("vector", dG)
            round1(o_w[3], o_vst + bsel * 1024, o_pl[3], xnt_of(par))
            tap_store("v", B(o_vst + bsel * 1024, 1024), c * 128)
            w("gpsimd", vE)
            vdma_prev[bsel] = dma("gpsimd", dG, vext_s[:, c * 1024:(c + 1) * 1024],
                                  B(o_vst + bsel * 1024, 1024))

            transpose_group([B(o_sqn + i * 128, 128) for i in range(8)], B(o_sqT + par * 1024, 1024))

            w("scalar", vE, hv_prev)
            w("tensor", sE, spv)
            mm(psB[:, 1552:1552 + H], Fv(o_btri, 128), Fv(o_sp, H), True, True)
            mm(psB[:, 1568:1568 + H], Fv(o_bones, 128), Fv(o_sp, H), True, True)
            ltv = mml(psB[:, 1584:1584 + H], Fv(o_bones + 256, 128), Fv(o_sp, H), True, True)
            w("scalar", tE, ltv)
            on("scalar", lambda e, o=Fv(o_eml + par * 32, H): e.activation(o, psB[:, 1552:1552 + H], AF.Exp))
            on("scalar", lambda e, o=Fv(o_el + par * 32, H): e.activation(o, psB[:, 1552:1552 + H], AF.Exp, scale=cst(-1.0)))
            on("scalar", lambda e: e.activation(Fv(o_ea, H), psB[:, 1568:1568 + H], AF.Exp, scale=cst(-1.0)))
            sev = stamp("scalar", sE, lambda e: e.activation(Fv(o_ea + 32, H), psB[:, 1584:1584 + H],
                                                             AF.Exp, scale=cst(-1.0)))
            sev_prev = sev
            w("vector", sE, sev)
            for sub in range(2):
                src = o_ea + sub * 32
                even = Fv(src, 16, 0, 64).rearrange("p (a b) -> p a b", b=2)[:, :, 0:1].rearrange("p a b -> p (a b)")
                odd = Fv(src, 16, 64, 128).rearrange("p (a b) -> p a b", b=2)[:, :, 1:2].rearrange("p a b -> p (a b)")
                stamp("vector", vE, lambda e, s=even, d=Fv(o_ep + par * 32 + sub * 16, 8, 0, 64): e.tensor_copy(d, s))
                stamp("vector", vE, lambda e, s=odd, d=Fv(o_ep + par * 32 + sub * 16, 8, 64, 128): e.tensor_copy(d, s))

        def back(c):
            nonlocal m_par, hv_prev, skv
            par = c % 2
            tap_sync("vector")
            round1(o_w[2], o_svn, o_pl[2], xnt_of(par))
            tap_store("sv", B(o_svn, 1024), c * 128)
            round1(o_w[1], o_skn, o_pl[1], xnt_of(par))
            vsync()
            skv = stamp("vector", vE, lambda e, eb=bcast_h(o_eml + par * 32): e.tensor_tensor(
                g3(B(o_sk2, 1024)), g3(B(o_skn, 1024)), eb, op=ALU.mult))
            tap_store("sk2", B(o_sk2, 1024), c * 128)
            w("tensor", vE, skv)
            transpose_group([B(o_sk2 + i * 128, 128) for i in range(8)], B(o_sk2T, 1024))

            w("tensor", vE, hv_prev)
            mv = 0
            for sub in range(2):
                r0 = sub * 64
                for hp_g in range(2):
                    for h in range(hp_g, H, 2):
                        j, hp = h // 2, h % 2
                        bank, ch = (0, h) if h < 8 else (1, h - 8)
                        mm(Bk(bank, r0, r0 + 64, ch * 64, ch * 64 + 64),
                           B(o_sqT + par * 1024 + j * 128 + r0, 64, hp * 64, hp * 64 + 64),
                           B(o_m + m_par * 512 + j * 64, 64, hp * 64, hp * 64 + 64),
                           h in (0, 8), False, tile=(hp * 64, r0))
                    on("tensor", lambda e: e.drain())
                for half in range(2):
                    tv = 0
                    for hp_g in range(2):
                        for hh in range(hp_g, 8, 2):
                            h = half * 8 + hh
                            j, hp = h // 2, h % 2
                            f = mml if hh == 7 else mm
                            tv = f(A(3, r0, r0 + 64, hh * 64, hh * 64 + 64),
                                   B(o_sk2T + j * 128 + r0, 64, hp * 64, hp * 64 + 64),
                                   B(o_sqT + par * 1024 + j * 128 + r0, 64, hp * 64, hp * 64 + 64),
                                   hh == 0, hh == 7, tile=(hp * 64, r0)) or tv
                        on("tensor", lambda e: e.drain())
                    w("vector", tE, tv)
                    mk = arena[r0:r0 + 64, o_mask:o_mask + 64].unsqueeze(1).broadcast_to([64, 8, 64])
                    pv = stamp("vector", vE, lambda e, o=g3(B(o_pp + half * 512, 512, r0, r0 + 64), 8),
                               i=g3(A(3, r0, r0 + 64), 8), m=mk: e.tensor_tensor(o, i, m, op=ALU.mult))
                    w("tensor", vE, pv)
                for h in range(H):
                    half, hh = h // 8, h % 8
                    bank, ch = (0, h) if h < 8 else (1, h - 8)
                    mm(Bk(bank, r0, r0 + 64, ch * 64, ch * 64 + 64),
                       B(o_pp + half * 512 + hh * 64, 64, r0, r0 + 64),
                       B(o_svn + h * 64, 64, r0, r0 + 64),
                       False, h in (7, 15), tile=(r0, r0))
                stv = 0
                for h in range(H):
                    j, hp = h // 2, h % 2
                    f = mml if h == 15 else mm
                    stv = f(Bk(2, hp * 64, hp * 64 + 64, j * 64, j * 64 + 64),
                            B(o_sk2 + h * 64, 64, r0, r0 + 64),
                            B(o_svn + h * 64, 64, r0, r0 + 64),
                            h in (0, 1), h in (14, 15), tile=(r0, hp * 64)) or stv
                w("vector", tE, stv)
                w("vector", vE)
                stamp("vector", vE, lambda e, mp=m_par: e.tensor_tensor(
                    Fv(o_pp, 512), Bk(2), B(o_m + mp * 512, 512), op=ALU.add))
                vsync()
                ep_b = Fv(o_ep + par * 32 + sub * 16, 8).unsqueeze(2).broadcast_to([128, 8, 64])
                mv = stamp("vector", vE, lambda e, np_=1 - m_par, eb=ep_b: e.tensor_tensor(
                    g3(B(o_m + np_ * 512, 512), 8), g3(Fv(o_pp, 512), 8), eb, op=ALU.mult))
                w("tensor", vE, mv)
                m_par = 1 - m_par
            hv_prev = stamp("vector", vE, lambda e, eb=bcast_h(o_el + par * 32): e.tensor_tensor(
                g3(B(o_hn, 1024)), g3(psB[:, 0:1024]), eb, op=ALU.mult))
            tap_store("hseq", B(o_hn, 1024), c * 128)
            w("tensor", vE, hv_prev)
            transpose_group([B(o_hn + i * 128, 128) for i in range(8)], B(o_big + c * 1024, 1024))

        for c in range(NCH + 1):
            if c < NCH:
                frontA(c)
            if c >= 1:
                back(c - 1)
            if c < NCH:
                frontB(c)

        if STOP_PHASE is not None and STOP_PHASE < 2:
            _en[0] = False
        prep_weightT(wd["Wq"], o_w2[0])
        prep_weightT(wd["Wk"], o_w2[1])
        build_plane(wd["bq"], o_pl[0], C)
        build_plane(wd["bk"], o_pl[1], C)
        w("gpsimd", vE)
        with nc.allow_non_contiguous_dma("fold [128,1] maxabs to one row"):
            gmx = dma("gpsimd", dG, Fv(o_small + 128, 128, 0, 1), Fv(o_kmax, 1))
        w("vector", dG)
        rmx = stamp("vector", vE, lambda e: e.tensor_reduce(
            Fv(o_kmax + 2, 1, 0, 1), Fv(o_small + 128, 128, 0, 1), axis=AX.X, op=ALU.max))
        w("tensor", vE, rmx)
        mm(psA[:, 1536:1537], Fv(o_onef, 128, 0, 1), Fv(o_kmax + 2, 1, 0, 1), True, True)
        tvm = tdrain()
        w("vector", tE, tvm)
        stamp("vector", vE, lambda e: e.tensor_copy(Fv(o_kmax + 8, 1), psA[:, 1536:1537]))
        vsync()
        stamp("vector", vE, lambda e: e.reciprocal(Fv(o_kmax + 12, 1), Fv(o_kmax + 8, 1)))
        vsync()
        stamp("vector", vE, lambda e: e.tensor_scalar(Fv(o_kmax + 12, 1), Fv(o_kmax + 12, 1), 7.0, None, op0=ALU.mult))
        vsync()
        fsc = stamp("vector", vE, lambda e: e.tensor_scalar(Fv(o_kmax + 8, 1), Fv(o_kmax + 8, 1),
                                                            1.0 / 7.0, None, op0=ALU.mult))

        qdma_prev = 0
        vload = [0] * NCH
        rnd = [0] * NCH
        krnd = [0] * NCH

        def norm_path(pfull, plane, rawo, rso, outo, fold_cb, tgate):
            tap_sync("vector")
            first = stamp("vector", vE, lambda e, pf=pfull, pl=B(plane, 1024):
                          e.tensor_tensor(B(rawo, 1024), pf, pl, op=ALU.add))
            vsync()
            stamp("vector", vE, lambda e: e.tensor_tensor(Fv(o_sk2, 1024), B(rawo, 1024), B(rawo, 1024), op=ALU.mult))
            vsync()
            rv = stamp("vector", vE, lambda e: e.tensor_reduce(
                Fv(rso, H), Fv(o_sk2, 1024).rearrange("p (a b) -> p a b", a=H), axis=AX.X, op=ALU.add))
            w("scalar", vE, rv)
            nl = stamp("scalar", sE, lambda e: e.activation(Fv(rso + 32, H), Fv(rso, H), AF.Ln, bias=cst(1e-19)))
            w("scalar", sE, nl)
            sv2 = stamp("scalar", sE, lambda e: e.activation(Fv(rso, H), Fv(rso + 32, H), AF.Exp, scale=cst(-0.5)))
            w("vector", sE, sv2)
            if fold_cb:
                stamp("vector", vE, lambda e: e.tensor_tensor(Fv(rso, H), Fv(rso, H), Fv(o_cb, H), op=ALU.mult))
                vsync()
            w("vector", tE, tgate)
            nv = stamp("vector", vE, lambda e: e.tensor_tensor(
                g3(B(outo, 1024)), g3(B(rawo, 1024)), bcast_h(rso), op=ALU.mult))
            return first, nv

        frdK2 = [0, 0]
        frdQ = 0
        nk = nq = 0
        tkv = tqv2 = 0
        for c in range(NCH + 1):
            if c >= 1:
                c1 = c - 1
                w("vector", tE, krnd[c1])
                kfull = psA[:, 0:1024] if c1 % 2 == 0 else psB[:, 1024:2048]
                frdK2[c1 % 2], nk = norm_path(kfull, o_pl[1], o_sqn, o_rstd, o_hn, True, tkv)
                w("vector", tE, rnd[c1])
                frdQ, nq = norm_path(psB[:, 0:1024], o_pl[0], o_skn, o_rstd + 64, o_hn2, False, tqv2)
            if c < NCH:
                w("tensor", vE, frdK2[c % 2])
                kv = 0
                for kc in range(8):
                    for hf in range(2):
                        f = mml if (kc == 7 and hf == 1) else mm
                        dst = A(hf) if c % 2 == 0 else Bk(2 + hf)
                        kv = f(dst, B(o_big + c * 1024 + kc * 128, 128),
                               B(o_w[1] + kc * 1024 + hf * 512, 512), kc == 0, kc == 7) or kv
                krnd[c] = kv
                w("tensor", vE, frdQ)
                tv = 0
                for kc in range(8):
                    for hf in range(2):
                        last = (kc == 7 and hf == 1)
                        f = mml if last else mm
                        tv = f(Bk(hf), B(o_big + c * 1024 + kc * 128, 128),
                               B(o_w[0] + kc * 1024 + hf * 512, 512), kc == 0, kc == 7) or tv
                rnd[c] = tv
            if c >= 1:
                c1 = c - 1
                w("tensor", vE, nk)
                tap_store("kc", B(o_hn, 1024), c1 * 128)
                tkv, _ = transpose_group([B(o_hn + i * 128, 128) for i in range(8)], B(o_kt + c1 * 1024, 1024))
                w("tensor", vE, nq)
                tap_store("qn", B(o_hn2, 1024), c1 * 128)
                w("vector", dG, qdma_prev)
                tqv2, _ = transpose_group([B(o_hn2 + i * 128, 128) for i in range(8)], B(o_sqT, 1024))
                tb_, tof = c1 // 4, (c1 % 4) * 128
                qdst = qt_s[:, tb_ * 4096:(tb_ + 1) * 4096].rearrange("p (a b) -> p a b", b=512)[:, :, tof:tof + 128]
                w("gpsimd", vE)
                qdma_prev = dma("gpsimd", dG, qdst, B(o_sqT, 1024).rearrange("p (a b) -> p a b", b=128))
                w("sync", tE, rnd[c1])
                w("sync", dG)
                vload[c1] = dma("sync", dS, arena[:, o_big + c1 * 1024:o_big + (c1 + 1) * 1024],
                                vext_s[:, c1 * 1024:(c1 + 1) * 1024])

        if STOP_PHASE is not None and STOP_PHASE < 3:
            _en[0] = False
        prep_weightT(wd["Wo"], o_w2[0], stg=o_p3, stg2=o_rv)
        build_plane(wd["bo"], o_pl3, C, brow=o_p3)
        for eng in Prog.ENGINES:
            w(eng, tE)
            w(eng, vE)
            w(eng, sE)
            w(eng, dG)
            w(eng, dS)
            w(eng, dX)
            w(eng, dF)
            w(eng, dT)
        dma("sync", dS, Fv(o_ind, 256), ind_d)
        w("tensor", dS)
        if "vx" in tap_d:
            dma("gpsimd", dT, tap_d["vx"], B(o_big, T * 8))
        on("vector", lambda e: e.memset(Fv(o_rv, 1024), 0.0))
        rvz = stamp("vector", vE, lambda e: e.memset(B(o_aot, 4096), 0.0))
        w("tensor", vE, rvz)

        qld = [0] * NTB
        qsem = [dX, dF]
        for tb in range(min(2, NTB)):
            qld[tb] = dma("sync", qsem[tb % 2], B(o_qtb + (tb % 2) * 4096, 4096), qt_s[:, tb * 4096:(tb + 1) * 4096])
        prev_exp_par = [0, 0]
        pp_attnv = [0, 0, 0, 0]
        adma_prev = [0, 0]
        st3 = {"aot_par": [0, 0], "rec_par": [0, 0], "bk3": 0, "aot_last": 0}

        def emit_wo_piece(tb, piece):
            t1, hf = piece >> 1, piece & 1
            w("tensor", vE, st3["bk3"])
            if piece == 0:
                w("tensor", vE, st3["aot_last"])
            wov = 0
            for j in range(8):
                f = mml if j == 7 else mm
                wov = f(Bk(3), B(o_aot + j * 512 + t1 * 128, 128),
                        B(o_w2[0] + j * 1024 + hf * 512, 512), j == 0, j == 7) or wov
            w("vector", tE, wov)
            if hf == 0:
                w("vector", dG, adma_prev[t1 % 2])
            ev = stamp("vector", vE, lambda e, o=B(o_att + (t1 % 2) * 1024 + hf * 512, 512),
                       i=Bk(3), pl=B(o_pl3 + hf * 512, 512): e.tensor_tensor(o, i, pl, op=ALU.add))
            st3["bk3"] = ev
            if hf == 1:
                tap_sync("vector")
                tap_store("attn", B(o_att + (t1 % 2) * 1024, 1024), tb * 512 + t1 * 128)
                w("gpsimd", vE, ev)
                w("gpsimd", dG)
                adma_prev[t1 % 2] = dma("gpsimd", dG,
                                        attn_s[tb * 512 + t1 * 128:tb * 512 + (t1 + 1) * 128, :],
                                        B(o_att + (t1 % 2) * 1024, 1024))

        def emit_wo(tb):
            for piece in range(8):
                emit_wo_piece(tb, piece)

        def emit_epi(g, av_last):
            par = g & 1
            abk = 0 if par == 0 else 2
            pr = g & 7
            w("tensor", vE, st3["rec_par"][par])
            w("tensor", vE, st3["bk3"])
            bc = mml(Bk(3), Fv(o_ind + par * 256, 128), Fv(o_rv + par * 1024, 512), True, True)
            w("vector", tE, bc)
            w("vector", vE, st3["aot_last"])
            tap_sync("vector")
            bcv = stamp("vector", vE, lambda e, o=Fv(o_bcv, 512), i=Bk(3): e.tensor_copy(o, i))
            st3["bk3"] = bcv
            w("vector", vE, bcv)
            aot_ev = stamp("vector", vE, lambda e, o=B(o_aot + pr * 512, 512),
                           i0=Bk(abk), i1=Fv(o_bcv, 512): e.tensor_tensor(o, i0, i1, op=ALU.mult))
            st3["aot_par"][par] = aot_ev
            st3["aot_last"] = aot_ev

        av_prev = 0
        NG = NTB * 8
        hoist_scv = [0, 0]

        def emit_scores(g2, s2):
            tb2, pr2 = g2 >> 3, g2 & 7
            qb2 = (tb2 % 2) * 4096
            sb2 = s2 % 2
            w("tensor", sE, prev_exp_par[sb2])
            scv = 0
            for hp in range(2):
                f = mml if hp == 1 else mm
                scv = f(psA[:, sb2 * 1024 + hp * 512:sb2 * 1024 + hp * 512 + 512],
                        B(o_kt + s2 * 1024 + pr2 * 128, 128, hp * 64, hp * 64 + 64),
                        B(o_qtb + qb2 + pr2 * 512, 512, hp * 64, hp * 64 + 64),
                        True, True, tile=(hp * 64, 0)) or scv
            return scv

        for g in range(NG):
            tb, pr = g >> 3, g & 7
            par = g & 1
            abk = 0 if par == 0 else 2
            rbase = par * 32
            if pr == 0 and g == 0:
                w("tensor", qsem[tb % 2], qld[tb])
            tap_sync("scalar")
            tap_sync("vector")
            bk0_gate = st3["aot_par"][par]
            bk1_gate = st3["rec_par"][par]
            exp_hist = [0, 0, 0, 0]
            for s in range(NST + 2):
                if s < NST:
                    sb = s % 2
                    if g > 0 and s < 2:
                        scv = hoist_scv[s]
                    else:
                        scv = emit_scores(g, s)
                    w("scalar", tE, scv)
                    w("scalar", tE, pp_attnv[s % 4])
                    exp_s = stamp("scalar", sE, lambda e, o=B(o_p3 + (s % 4) * 1024, 1024),
                                  i=psA[:, sb * 1024:sb * 1024 + 1024]: e.activation(o, i, AF.Exp))
                    prev_exp_par[sb] = exp_s
                    exp_hist[s % 4] = exp_s
                if s >= 2:
                    s1 = s - 2
                    sb1 = s1 % 4
                    w("tensor", sE, exp_hist[sb1])
                    if s1 == 0:
                        w("tensor", vE, bk0_gate)
                    for hp in range(2):
                        h = pr * 2 + hp
                        mm(Bk(abk, hp * 64, hp * 64 + 64),
                           arena[:, o_big + s1 * 1024 + h * 64:o_big + s1 * 1024 + h * 64 + 64],
                           B(o_p3 + sb1 * 1024 + hp * 512, 512),
                           s1 == 0, s1 == NST - 1, tile=(0, hp * 64))
                    if s1 == 0:
                        w("tensor", vE, bk1_gate)
                    for hp in range(2):
                        f = mml if hp == 1 else mm
                        pp_attnv[sb1] = f(Bk(1, rbase + hp * 64, rbase + hp * 64 + 1),
                                          arena[:, o_onec:o_onec + 1],
                                          B(o_p3 + sb1 * 1024 + hp * 512, 512),
                                          s1 == 0, s1 == NST - 1,
                                          tile=(0, rbase + hp * 64)) or pp_attnv[sb1]
                if pr == 1 and tb >= 1 and 0 < s <= NST and (s * 8) % NST == 0:
                    emit_wo_piece(tb - 1, s * 8 // NST - 1)
                if s == max(2, (3 * NST) // 8) and g >= 1 and not (pr == 1 and tb >= 1):
                    emit_epi(g - 1, av_prev)
                if s == NST - 1 and g + 1 < NG:
                    if (g + 1) & 7 == 0:
                        w("tensor", qsem[((g + 1) >> 3) % 2], qld[(g + 1) >> 3])
                    for s2 in (0, 1):
                        hoist_scv[s2] = emit_scores(g + 1, s2)
            av_last = pp_attnv[(NST - 1) % 4]
            if g >= 1 and pr == 1 and tb >= 1:
                emit_epi(g - 1, av_prev)
            w("vector", tE, av_last)
            rc = 0
            for rr in (rbase, rbase + 64):
                rc = stamp("vector", vE, lambda e, i=Bk(1, rr, rr + 1),
                           o=Fv(o_rv + par * 1024, 512, rr, rr + 1): e.reciprocal(o, i))
            if pr == 3 and tb + 2 < NTB:
                w("sync", tE)
                qld[tb + 2] = dma("sync", qsem[tb % 2], B(o_qtb + (tb % 2) * 4096, 4096),
                                  qt_s[:, (tb + 2) * 4096:(tb + 3) * 4096])
            av_prev = av_last
            st3["rec_par"][par] = rc
        emit_epi(NTB * 8 - 1, av_prev)
        emit_wo(NTB - 1)

        if STOP_PHASE is not None and STOP_PHASE < 4:
            _en[0] = False
        for eng in Prog.ENGINES:
            w(eng, tE)
            w(eng, vE)
            w(eng, sE)
            w(eng, dG)
            w(eng, dS)
            w(eng, dX)
            w(eng, dF)
            w(eng, dT)
        dva = dma("sync", dS, Fv(o_fst, 128).rearrange("p (a b) -> p a b", b=16),
                  wd["fa"].rearrange("(a p) r -> p a r", p=128))
        dvb = dma("sync", dS, Fv(o_fst + 2048, 1024, 0, 16), wd["fb"])
        w("vector", dS, dvb)
        on("vector", lambda e: e.tensor_copy(B(o_xn2, 128), Fv(o_fst, 128)))
        cvb = stamp("vector", vE, lambda e: e.tensor_copy(B(o_fbB, 1024, 0, 16), Fv(o_fst + 2048, 1024, 0, 16)))
        w("tensor", vE, cvb)
        for i in range(8):
            on("tensor", lambda e, s=B(o_xn2 + i * 16, 16), o=A2bf(i * 128, (i + 1) * 128, 0, 16),
               f=(i == 0), l=(i == 7), ид=arena[:, o_id:o_id + 128]:
               e.matmul(o, s, ид, is_transpose=True, start=f, stop=l, skip_group_check=True))
        tva = tdrain()
        w("vector", tE, tva)
        ave = stamp("vector", vE, lambda e: e.tensor_copy(B(o_aT, 1024, 0, 16), A2bf(0, 1024, 0, 16)))
        w("tensor", vE, ave)
        build_plane(wd["fbias"], o_pl4, C, brow=o_x4a)
        for kc in range(8):
            bsel = kc % 2
            w("sync", vE)
            dv = dma("sync", dS, Fv(o_fst + bsel * 2048, 1024), wd["fk"][kc * 128:(kc + 1) * 128, :])
            w("vector", dS, dv)
            w("vector", vE, fsc)
            w("vector", vE)
            stamp("vector", vE, lambda e, i=Fv(o_fst + bsel * 2048, 1024): e.tensor_scalar(
                Fv(o_wq4, 1024), i, Fv(o_kmax + 12, 1), MAGIC, op0=ALU.mult, op1=ALU.add))
            vsync()
            stamp("vector", vE, lambda e: e.tensor_scalar(
                Fv(o_wq4, 1024), Fv(o_wq4, 1024), -MAGIC, Fv(o_kmax + 8, 1), op0=ALU.add, op1=ALU.mult))
            tv = 0
            for hf in range(2):
                f = mml if hf == 1 else mm
                tv = f(A(hf), B(o_aT + kc * 128, 128, 0, 16), B(o_fbB + hf * 512, 512, 0, 16), True, True) or tv
            w("vector", tE, tv)
            vsync()
            wfv = stamp("vector", vE, lambda e, d=B(o_w2[1] + kc * 1024, 1024):
                        e.tensor_tensor(d, psA[:, 0:1024], Fv(o_wq4, 1024), op=ALU.add))
            w("tensor", vE, wfv)
        x4a = [0] * NCH
        x4b = [0] * NCH
        ln4 = [0] * NCH
        ln4se = [0] * NCH
        w("sync", tE)
        w("sync", vE)
        for c in range(min(2, NCH)):
            x4a[c] = dma("sync", dX, Fv(o_x4a + (c % 2) * 2048, 1024), x_d[c * 128:(c + 1) * 128, :])
            x4b[c] = dma("sync", dF, B(o_x4b + (c % 2) * 1024, 1024), attn_s[c * 128:(c + 1) * 128, :])
        odma_prev = [0, 0]
        out_dma = 0
        yfree = 0
        xnT_gate = 0
        w("vector", tE, tva)
        for c in range(NCH + 1):
            if c < NCH:
                bsel = c % 2
                tap_sync("vector")
                tap_sync("sync")
                w("vector", dX)
                w("vector", dF)
                x1v = stamp("vector", vE, lambda e, a=Fv(o_x4a + bsel * 2048, 1024),
                            b=B(o_x4b + bsel * 1024, 1024): e.tensor_tensor(a, a, b, op=ALU.add))
                tap_store("x1", Fv(o_x4a + bsel * 2048, 1024), c * 128)
                vsync()
                ln4se[c] = ln_stats(o_x4a + bsel * 2048, o_s4 + bsel * 32, o_lnsq4)
            if c >= 1:
                c1 = c - 1
                b1 = c1 % 2
                ln4[c1] = ln_norm(o_x4a + b1 * 2048, o_xn2 + b1 * 1024, o_s4 + b1 * 32, ln4se[c1])
                w("tensor", vE, ln4[c1])
                w("vector", tE, xnT_gate)
                transpose_group([B(o_xn2 + b1 * 1024 + i * 128, 128) for i in range(8)], B(o_x2T, 1024))
                w("tensor", vE, yfree)
                ytv = 0
                for kc in range(8):
                    for hf in range(2):
                        last = (kc == 7 and hf == 1)
                        f = mml if last else mm
                        ytv = f(A(hf), B(o_x2T + kc * 128, 128), B(o_w2[1] + kc * 1024 + hf * 512, 512),
                                kc == 0, kc == 7) or ytv
                xnT_gate = ytv
                w("vector", tE, ytv)
                w("vector", dG, odma_prev[b1])
                yfree = stamp("vector", vE, lambda e, o=Fv(o_ou + b1 * 2048, 1024): e.tensor_tensor(
                    o, psA[:, 0:1024], B(o_pl4, 1024), op=ALU.add))
                vsync()
                ov_ = stamp("vector", vE, lambda e, o=Fv(o_ou + b1 * 2048, 1024),
                            x1=Fv(o_x4a + b1 * 2048, 1024): e.tensor_tensor(o, o, x1, op=ALU.add))
                w("gpsimd", vE, ov_)
                w("gpsimd", dG)
                out_dma = dma("gpsimd", dG, out_d[c1 * 128:(c1 + 1) * 128, :], Fv(o_ou + b1 * 2048, 1024))
                odma_prev[b1] = out_dma
                if c1 + 2 < NCH:
                    tap_sync("sync")
                    w("sync", vE, ov_)
                    x4a[c1 + 2] = dma("sync", dX, Fv(o_x4a + b1 * 2048, 1024),
                                      x_d[(c1 + 2) * 128:(c1 + 3) * 128, :])
                    x4b[c1 + 2] = dma("sync", dF, B(o_x4b + b1 * 1024, 1024),
                                      attn_s[(c1 + 2) * 128:(c1 + 3) * 128, :])
        w("gpsimd", dG, out_dma)

        blk = ctx.enter_context(nc.Block())
        p.run(blk)
    return nc


def make_consts():
    ident = np.eye(128, dtype=ml_dtypes.bfloat16)
    u = np.arange(128)
    btri = ((u[:, None] // 64 == u[None, :] // 64) & (u[:, None] <= u[None, :])).astype(np.float32)
    bones = np.zeros((128, 256), np.float32)
    bones[0:64, 0:128] = 1.0
    bones[64:128, 128:256] = 1.0
    m = (np.arange(64)[:, None] <= np.arange(64)[None, :]).astype(np.float32)
    mask2 = np.concatenate([m, m], 0).astype(ml_dtypes.bfloat16)
    ind = np.zeros((128, 256), np.float32)
    ind[0, 0:64] = 1.0
    ind[64, 64:128] = 1.0
    ind[32, 128:192] = 1.0
    ind[96, 192:256] = 1.0
    identf = np.eye(128, dtype=np.float32)
    return {"ident": ident, "btri": btri, "bones": bones, "mask2": mask2, "ind": ind,
            "identf": identf}


def make_in_map(xb, inputs):
    m = {"x": np.ascontiguousarray(np.asarray(xb, np.float32))}
    ren = {"Wsq": "L0_attn_Wsq", "bsq": "L0_attn_bsq", "Wsk": "L0_attn_Wsk", "bsk": "L0_attn_bsk",
           "Wsv": "L0_attn_Wsv", "bsv": "L0_attn_bsv", "Wg": "L0_attn_Wg", "bg": "L0_attn_bg",
           "Wq": "L0_attn_Wq", "bq": "L0_attn_bq", "Wk": "L0_attn_Wk", "bk": "L0_attn_bk",
           "Wv": "L0_attn_Wv", "bv": "L0_attn_bv", "Wo": "L0_attn_Wo", "bo": "L0_attn_bo",
           "temp": "L0_attn_temp", "fk": "L0_ffn_kernel", "fbias": "L0_ffn_bias",
           "fa": "L0_ffn_a", "fb": "L0_ffn_b"}
    for k, src in ren.items():
        v = np.asarray(inputs[src], np.float32)
        if k in ("bsq", "bsk", "bsv", "bq", "bk", "bv", "bo", "fbias", "bg", "temp"):
            v = v.reshape(1, -1)
        m[k] = np.ascontiguousarray(v)
    m.update(make_consts())
    return m


def kernel(**inputs):
    x = np.asarray(inputs["L0_attn_x"], np.float32)
    Bn, T, _ = x.shape
    nc = build(T=T)
    in_maps = [make_in_map(x[b], inputs) for b in range(Bn)]
    res = run_bass_kernel_spmd(nc, in_maps, core_ids=list(range(Bn)))
    return np.stack([np.asarray(r["out"], np.float32) for r in res.results], 0)


# version 27 (final)
# speedup vs baseline: 1.0062x; 1.0062x over previous
import contextlib
import numpy as np
import ml_dtypes

import concourse.bass as bass
import concourse.mybir as mybir
import bass_rust
from concourse import library_config
from concourse.bass_utils import run_bass_kernel_spmd

F32 = mybir.dt.float32
BF16 = mybir.dt.bfloat16
AF = mybir.ActivationFunctionType
ALU = mybir.AluOpType
AX = mybir.AxisListType

C = 1024
H = 16
MAGIC = float(np.float32(3 * 2 ** 22))


class Ctr:
    def __init__(self, h):
        self.h = h
        self.v = 0


class Prog:
    ENGINES = ("sync", "gpsimd", "vector", "scalar", "tensor")

    def __init__(self):
        self.q = {e: [] for e in self.ENGINES}

    def run(self, block):
        for eng in self.ENGINES:
            fns = self.q[eng]
            getattr(block, eng)(lambda e, fns=fns: [f(e) for f in fns])


STOP_PHASE = None


def build(T=4096, taps=()):
    nc = bass.Bass("TRN2", target_bir_lowering=False)
    NCH = T // 128
    NTB = T // 512
    NST = T // 128

    x_d = nc.dram_tensor("x", [T, C], F32, kind="ExternalInput").ap()
    wd = {}
    for nm in ("Wsq", "Wsk", "Wsv", "Wq", "Wk", "Wv", "Wo", "fk"):
        wd[nm] = nc.dram_tensor(nm, [C, C], F32, kind="ExternalInput").ap()
    for nm in ("bsq", "bsk", "bsv", "bq", "bk", "bv", "bo", "fbias"):
        wd[nm] = nc.dram_tensor(nm, [1, C], F32, kind="ExternalInput").ap()
    wd["Wg"] = nc.dram_tensor("Wg", [H, C], F32, kind="ExternalInput").ap()
    wd["bg"] = nc.dram_tensor("bg", [1, H], F32, kind="ExternalInput").ap()
    wd["temp"] = nc.dram_tensor("temp", [1, H], F32, kind="ExternalInput").ap()
    wd["fa"] = nc.dram_tensor("fa", [C, 16], F32, kind="ExternalInput").ap()
    wd["fb"] = nc.dram_tensor("fb", [16, C], F32, kind="ExternalInput").ap()
    ident_d = nc.dram_tensor("ident", [128, 128], BF16, kind="ExternalInput").ap()
    btri_d = nc.dram_tensor("btri", [128, 128], F32, kind="ExternalInput").ap()
    bones_d = nc.dram_tensor("bones", [128, 256], F32, kind="ExternalInput").ap()
    mask2_d = nc.dram_tensor("mask2", [128, 64], BF16, kind="ExternalInput").ap()
    ind_d = nc.dram_tensor("ind", [128, 256], F32, kind="ExternalInput").ap()
    idf_d = nc.dram_tensor("identf", [128, 128], F32, kind="ExternalInput").ap()
    out_d = nc.dram_tensor("out", [T, C], F32, kind="ExternalOutput").ap()

    vext_s = nc.dram_tensor("vext_s", [128, T * 8], BF16, kind="Internal").ap()
    qt_s = nc.dram_tensor("qt_s", [128, T * 8], BF16, kind="Internal").ap()
    attn_s = nc.dram_tensor("attn_s", [T, C], BF16, kind="Internal").ap()

    tshape = {"xn": [T, C], "sq": [T, C], "sk2": [T, C], "sv": [T, C], "v": [T, C],
              "sp": [T, H], "hseq": [T, C], "kc": [T, C], "qn": [T, C],
              "attn": [T, C], "x1": [T, C], "p3": [128, 2048], "aot": [128, 4096],
              "rv": [128, 1024], "vx": [128, T * 8], "ovp": [128, 2048], "wot": [128, 8192], "wop": [128, 2048]}
    tap_d = {t_: nc.dram_tensor("tap_" + t_, tshape[t_], BF16 if t_ in ("attn",) else F32,
                                kind="ExternalOutput").ap()
             for t_ in taps}

    p = Prog()
    _en = [True]

    def on(eng, fn):
        if _en[0]:
            p.q[eng].append(fn)

    ctx = contextlib.ExitStack()
    with ctx:
        AR = 98304
        arena = ctx.enter_context(nc.sbuf_tensor("arena", [128, AR], BF16))

        def B(off, n, p0=0, p1=128):
            return arena[p0:p1, off:off + n]

        def Fv(off, n, p0=0, p1=128):
            assert off % 2 == 0
            return arena[p0:p1, off:off + 2 * n].bitcast(F32)

        def cst(v, p0=0, p1=128):
            return Fv(o_cst + 2 * CSTV[v], 1, p0, p1)

        class Alloc:
            def __init__(self, base):
                self.o = base

            def __call__(self, n):
                self.o = (self.o + 1) // 2 * 2
                r = self.o
                self.o += n
                assert self.o <= AR, self.o
                return r

        g_al = Alloc(0)
        o_big = g_al(T * 8)
        KTR = max(T * 8, 32768)
        o_ktr = g_al(KTR)
        o_w2 = [g_al(8192), g_al(8192)]
        o_id = g_al(128)
        o_cb = g_al(32)
        o_onec = g_al(2)
        o_oner = g_al(128)
        o_kmax = g_al(16)
        o_cst = g_al(32)
        o_spc = g_al(4)
        o_mask = g_al(64)
        o_idf = g_al(256)
        o_onef = g_al(256)
        OVB = g_al.o
        CSTV = {1e-5: 0, 1.0: 1, 1e-19: 2, -0.5: 3, -1.0: 4}

        o_kt = o_ktr
        k1 = Alloc(o_ktr)
        o_wsv = k1(8192)
        o_wv = k1(8192)
        o_xst = k1(2 * 2048)
        o_wst = k1(2 * 2048)
        o_lnsq = k1(2048)
        o_vst = k1(2 * 1024)
        o_m = k1(2 * 512)
        o_pp = k1(1024)
        o_xnn = k1(1024)
        o_xnT = k1(1024)
        assert k1.o <= o_ktr + KTR, k1.o
        o_w = [o_w2[0], o_w2[1], o_wsv, o_wv]

        a1 = Alloc(OVB)
        o_pl = [a1(1024) for _ in range(4)]
        o_plg = a1(16)
        o_btri = a1(256)
        o_bones = a1(512)
        o_wg = a1(128)
        o_small = a1(256)
        o_sqn = a1(1024)
        o_skn = a1(1024)
        o_sk2 = a1(1024)
        o_svn = a1(1024)
        o_sqT = a1(2 * 1024)
        o_sk2T = a1(1024)
        o_hn = a1(1024)
        o_hn2 = a1(1024)
        o_xnT2 = o_hn2
        o_sp = a1(32)
        o_eml = a1(64)
        o_el = a1(64)
        o_ea = a1(64)
        o_ep = a1(64)
        o_st1 = a1(64)
        o_st2 = a1(128)
        o_rstd = a1(128)
        o_brow = o_xst
        p1_end = a1.o

        a3 = Alloc(OVB)
        o_p3 = a3(2 * 2048)
        o_aot = a3(4096)
        o_rv = a3(2048)
        o_bcv = a3(1024)
        o_att = a3(2 * 1024)
        o_ind = a3(512)
        o_pl3 = a3(1024)
        o_qtb = o_w2[1]
        p3_end = a3.o

        a4 = Alloc(o_ktr)
        o_x4a = a4(2 * 2048)
        o_x4b = a4(2 * 1024)
        o_fst = a4(2 * 2048)
        o_wq4 = a4(2048)
        o_xn2 = a4(2 * 1024)
        o_x2T = a4(1024)
        o_aT = a4(1024)
        o_fbB = a4(1024)
        o_ou = a4(2 * 2048)
        o_s4 = a4(64)
        o_lnsq4 = a4(2048)
        o_pl4 = a4(1024)
        assert a4.o <= o_ktr + KTR, a4.o
        assert max(p1_end, p3_end) <= AR, (p1_end, p3_end, AR)

        psA = ctx.enter_context(nc.psum_tensor("psA", [128, 2048], F32))
        psB = ctx.enter_context(nc.psum_tensor("psB", [128, 2048], F32))
        psA_bf = psA.ap().bitcast(BF16)

        def A(i, p0=0, p1=128, c0=0, c1=512):
            return psA[p0:p1, i * 512 + c0:i * 512 + c1]

        def Bk(i, p0=0, p1=128, c0=0, c1=512):
            return psB[p0:p1, i * 512 + c0:i * 512 + c1]

        def A2bf(c0, c1, p0=0, p1=128):
            return psA_bf[p0:p1, 2048 + c0:2048 + c1]

        dS = Ctr(ctx.enter_context(nc.semaphore("dS")))
        dG = Ctr(ctx.enter_context(nc.semaphore("dG")))
        tE = Ctr(ctx.enter_context(nc.semaphore("tE")))
        vE = Ctr(ctx.enter_context(nc.semaphore("vE")))
        sE = Ctr(ctx.enter_context(nc.semaphore("sE")))
        gC = Ctr(ctx.enter_context(nc.semaphore("gC")))
        dX = Ctr(ctx.enter_context(nc.semaphore("dX")))
        dT = Ctr(ctx.enter_context(nc.semaphore("dT")))
        dF = Ctr(ctx.enter_context(nc.semaphore("dF")))

        def w(eng, ctr, v=None):
            v = ctr.v if v is None else v
            if v > 0:
                on(eng, lambda e, h=ctr.h, v=v: e.wait_ge(h, v))

        def stamp(eng, ctr, fn):
            ctr.v += 1
            on(eng, lambda e, fn=fn, h=ctr.h: fn(e).then_inc(h, 1))
            return ctr.v

        def dma(eng, ctr, out, in_):
            ctr.v += 16
            on(eng, lambda e, o=out, i=in_, h=ctr.h: e.dma_start(out=o, in_=i).then_inc(h, 16))
            return ctr.v

        def vsync():
            w("vector", vE)

        def mm(out, lhsT, rhs, start, stop, tile=None):
            on("tensor", lambda e, o=out, l=lhsT, r=rhs, s=start, st=stop, tp=tile:
               e.matmul(o, l, r, start=s, stop=st, tile_position=tp, skip_group_check=True))

        def tdrain():
            return stamp("tensor", tE, lambda e: e.drain(fusable=False))

        def mml(out, lhsT, rhs, start, stop, tile=None):
            tE.v += 1
            on("tensor", lambda e, o=out, l=lhsT, r=rhs, st_=start, sp=stop, tp=tile, h=tE.h:
               e.matmul(o, l, r, start=st_, stop=sp, tile_position=tp, skip_group_check=True).then_inc(h, 1))
            return tE.v

        def bcast_h(off_f32, g=H, w_=64, p0=0, p1=128):
            return Fv(off_f32, g, p0, p1).unsqueeze(2).broadcast_to([p1 - p0, g, w_])

        def g3(ap, g=H):
            return ap.rearrange("p (a b) -> p a b", a=g)

        def transpose_group(srcs, dst, tw=128):
            n = len(srcs)
            for i, s in enumerate(srcs):
                if i < n - 1:
                    on("tensor", lambda e, s=s, o=A2bf(i * tw, (i + 1) * tw), f=(i == 0),
                       ид=arena[:, o_id:o_id + 128]: e.matmul(o, s, ид, is_transpose=True, start=f, stop=False, skip_group_check=True))
                else:
                    tE.v += 1
                    on("tensor", lambda e, s=s, o=A2bf(i * tw, (i + 1) * tw), f=(i == 0), h=tE.h:
                       e.matmul(o, s, arena[:, o_id:o_id + 128], is_transpose=True, start=f, stop=True,
                                skip_group_check=True).then_inc(h, 1))
            tv = tE.v
            w("vector", tE, tv)
            vE.v += 1
            src = A2bf(0, n * tw)
            if dst.ndim == 3:
                src = src.rearrange("p (a b) -> p a b", b=tw)
            on("vector", lambda e, d=dst, src=src, h=vE.h:
               e.tensor_copy(d, src).then_inc(h, 1))
            ev = vE.v
            w("tensor", vE, ev)
            return tv, ev

        def ln_chunk(xo, outo, so, sqo):
            se = ln_stats(xo, so, sqo)
            return ln_norm(xo, outo, so, se)

        def tap_store(name, ap, row0, n=128):
            if name in tap_d:
                w("gpsimd", vE)
                w("gpsimd", sE)
                w("gpsimd", dT)
                dma("gpsimd", dT, tap_d[name][row0:row0 + n, :], ap)

        def tap_sync(eng):
            if tap_d:
                w(eng, dT)

        def build_plane(bias_ap, plane_off, width, brow=None):
            brow = o_brow if brow is None else brow
            tap_sync("sync")
            tap_sync("vector")
            w("sync", vE)
            dv = dma("sync", dS, Fv(brow, width, 0, 1), bias_ap)
            w("vector", dS, dv)
            cv = stamp("vector", vE, lambda e, w_=width, br=brow:
                       e.tensor_copy(B(br + 2048, w_, 0, 1), Fv(br, w_, 0, 1)))
            w("tensor", vE, cv)
            nh = (width + 511) // 512
            for i in range(nh):
                wdt = min(512, width - i * 512)
                mm(psA[:, 1024 + i * 512:1024 + i * 512 + wdt],
                   arena[0:1, o_oner:o_oner + 128],
                   B(brow + 2048 + i * 512, wdt, 0, 1), True, True)
            tv = tdrain()
            w("vector", tE, tv)
            ev = stamp("vector", vE, lambda e, po=plane_off, w_=width:
                       e.tensor_copy(B(po, w_), psA[:, 1024:1024 + w_]))
            w("tensor", vE, ev)
            return ev

        def prep_weightT(w_ap, slot_off, stg=None, stg2=None):
            stg = o_wst if stg is None else stg
            stg2 = o_sqn if stg2 is None else stg2
            tap_sync("vector")
            tap_sync("sync")
            for kc in range(8):
                bsel = kc % 2
                w("sync", vE)
                dv = dma("sync", dS, Fv(stg + bsel * 2048, 1024), w_ap[kc * 128:(kc + 1) * 128, :])
                w("vector", dS, dv)
                cv = stamp("vector", vE, lambda e, b=bsel, g_=stg, g2=stg2:
                           e.tensor_copy(B(g2, 1024), Fv(g_ + b * 2048, 1024)))
                w("tensor", vE, cv)
                dst = B(slot_off, 8192).rearrange("p (a b) -> p a b", b=1024)[:, :, kc * 128:(kc + 1) * 128]
                transpose_group([B(stg2 + i * 128, 128) for i in range(8)], dst)

        def ln_stats(xo, so, sqo):
            stamp("vector", vE, lambda e, x=Fv(xo, 1024), o=Fv(so, 1): e.reduce_sum(o, x, axis=AX.X))
            stamp("vector", vE, lambda e, x=Fv(xo, 1024), o=Fv(sqo, 1024): e.tensor_tensor(o, x, x, op=ALU.mult))
            vsync()
            stamp("vector", vE, lambda e, x=Fv(sqo, 1024), o=Fv(so + 2, 1): e.reduce_sum(o, x, axis=AX.X))
            stamp("vector", vE, lambda e, i=Fv(so, 1), o=Fv(so + 4, 1): e.tensor_scalar(o, i, 1.0 / C, None, op0=ALU.mult))
            vsync()
            stamp("vector", vE, lambda e, i=Fv(so + 2, 1), o=Fv(so + 6, 1): e.tensor_scalar(o, i, 1.0 / C, None, op0=ALU.mult))
            stamp("vector", vE, lambda e, i=Fv(so + 4, 1), o=Fv(so + 8, 1): e.tensor_tensor(o, i, i, op=ALU.mult))
            vsync()
            sv_ = stamp("vector", vE, lambda e, a=Fv(so + 6, 1), b=Fv(so + 8, 1), o=Fv(so + 10, 1):
                        e.tensor_tensor(o, a, b, op=ALU.subtract))
            w("scalar", vE, sv_)
            sl = stamp("scalar", sE, lambda e, i=Fv(so + 10, 1), o=Fv(so + 12, 1):
                       e.activation(o, i, AF.Ln, bias=cst(1e-5)))
            w("scalar", sE, sl)
            se = stamp("scalar", sE, lambda e, i=Fv(so + 12, 1), o=Fv(so + 14, 1):
                       e.activation(o, i, AF.Exp, scale=cst(-0.5)))
            return se

        def ln_norm(xo, outo, so, se):
            w("vector", sE, se)
            vsync()
            rv = stamp("vector", vE, lambda e, x=Fv(xo, 1024), o=B(outo, 1024), m=Fv(so + 4, 1), r=Fv(so + 14, 1):
                       e.tensor_scalar(o, x, m, r, op0=ALU.subtract, op1=ALU.mult))
            return rv

        dma("sync", dS, arena[:, o_id:o_id + 128], ident_d)
        dma("sync", dS, Fv(o_btri, 128), btri_d)
        dma("sync", dS, Fv(o_bones, 256), bones_d)
        dma("sync", dS, arena[:, o_mask:o_mask + 64], mask2_d)
        dma("sync", dS, Fv(o_idf, 128), idf_d)
        dv_temp = dma("sync", dS, Fv(o_small, H, 0, 1), wd["temp"])
        on("vector", lambda e: e.memset(arena[0:1, o_oner:o_oner + 128], 1.0))
        on("vector", lambda e: e.memset(Fv(o_onef, 128, 0, 1), 1.0))
        on("vector", lambda e: e.memset(arena[:, o_onec:o_onec + 1], 1.0))
        on("vector", lambda e: e.memset(B(o_m, 1024), 0.0))
        for _v, _i in CSTV.items():
            on("vector", lambda e, o=Fv(o_cst + 2 * _i, 1), v=float(_v): e.memset(o, v))
        mz = stamp("vector", vE, lambda e: e.memset(Fv(o_kmax, 8), 0.0))
        w("tensor", dS)
        w("scalar", dS, dv_temp)
        w("scalar", vE, mz)
        pe1 = stamp("scalar", sE, lambda e: e.activation(Fv(o_small + 32, H, 0, 1), Fv(o_small, H, 0, 1), AF.Exp))
        w("scalar", sE, pe1)
        sp_t = stamp("scalar", sE, lambda e: e.activation(Fv(o_small + 64, H, 0, 1), Fv(o_small + 32, H, 0, 1),
                                                          AF.Ln, bias=cst(1.0, 0, 1)))
        w("vector", sE, sp_t)
        w("vector", vE, mz)
        cb_c = stamp("vector", vE, lambda e: e.tensor_copy(B(o_brow + 2048, H, 0, 1), Fv(o_small + 64, H, 0, 1)))
        w("tensor", vE, cb_c)
        mm(psA[:, 1536:1536 + H], arena[0:1, o_oner:o_oner + 128], B(o_brow + 2048, H, 0, 1), True, True)
        tv = tdrain()
        w("vector", tE, tv)
        cb_e = stamp("vector", vE, lambda e: e.tensor_copy(Fv(o_cb, H), psA[:, 1536:1536 + H]))
        w("tensor", vE, cb_e)

        for wnm, slot in (("Wsq", 0), ("Wsk", 1), ("Wsv", 2), ("Wv", 3)):
            prep_weightT(wd[wnm], o_w[slot])
        w("sync", vE)
        dvg = dma("sync", dS, Fv(o_wst, 1024, 0, 16), wd["Wg"])
        w("vector", dS, dvg)
        cvg = stamp("vector", vE, lambda e: e.tensor_copy(B(o_sqn, 1024, 0, 16), Fv(o_wst, 1024, 0, 16)))
        w("tensor", vE, cvg)
        for i in range(8):
            on("tensor", lambda e, s=B(o_sqn + i * 128, 128, 0, 16), o=A2bf(i * 16, (i + 1) * 16),
               f=(i == 0), l=(i == 7), ид=arena[0:16, o_id:o_id + 16]:
               e.matmul(o, s, ид, is_transpose=True, start=f, stop=l, skip_group_check=True))
        tv = tdrain()
        w("vector", tE, tv)
        wgv = stamp("vector", vE, lambda e: e.tensor_copy(B(o_wg, 128), A2bf(0, 128)))
        w("tensor", vE, wgv)
        for bnm, po, wd_ in (("bsq", o_pl[0], C), ("bsk", o_pl[1], C), ("bsv", o_pl[2], C),
                             ("bv", o_pl[3], C), ("bg", o_plg, H)):
            build_plane(wd[bnm], po, wd_)

        if STOP_PHASE is not None and STOP_PHASE < 1:
            _en[0] = False
        w("vector", vE)
        w("sync", vE)
        w("sync", sE)
        x_dma = [0] * NCH
        for c in range(min(2, NCH)):
            x_dma[c] = dma("sync", dX, Fv(o_xst + (c % 2) * 2048, 1024), x_d[c * 128:(c + 1) * 128, :])
        fk_red = [0] * 8
        fk_dma = [0] * 8
        vdma_prev = [0, 0]
        m_par = 0
        hv_prev = 0
        sev_prev = 0
        ln_done = [0] * NCH
        skv = 0

        def xnt_of(par):
            return o_xnT if par == 0 else o_xnT2

        def round1(wslot, dsto, plane, xnt_o, also_g=False):
            tv = 0
            for kc in range(8):
                for hf in range(2):
                    last = (kc == 7 and hf == 1 and not also_g)
                    f = mml if last else mm
                    tv = f(A(hf), B(xnt_o + kc * 128, 128), B(wslot + kc * 1024 + hf * 512, 512),
                           kc == 0, kc == 7) or tv
                if also_g:
                    f = mml if kc == 7 else mm
                    tv = f(psB[:, 1536:1536 + H], B(xnt_o + kc * 128, 128), B(o_wg + kc * 16, 16),
                           kc == 0, kc == 7) or tv
            w("vector", tE, tv)
            ev = stamp("vector", vE, lambda e, d=B(dsto, 1024), pl=B(plane, 1024):
                       e.tensor_tensor(d, psA[:, 0:1024], pl, op=ALU.add))
            w("tensor", vE, ev)
            return ev

        xnt_tv = [0]
        ln_se = [0]

        def frontA(c):
            bsel = c % 2
            tap_sync("vector")
            tap_sync("scalar")
            w("vector", dX)
            ln_se[0] = ln_stats(o_xst + bsel * 2048, o_st1, o_lnsq)

            if c < 8:
                if c >= 1:
                    w("sync", vE, fk_red[c - 1])
                else:
                    w("sync", vE)
                fk_dma[c] = dma("sync", dF, Fv(o_wst + bsel * 2048, 1024), wd["fk"][c * 128:(c + 1) * 128, :])
                w("vector", dF)
                w("vector", vE)
                stamp("vector", vE, lambda e, i=Fv(o_wst + bsel * 2048, 1024), o=Fv(o_st2 + 96, 1):
                      e.tensor_reduce(o, i, axis=AX.X, op=ALU.max, apply_absolute_value=True))
                vsync()
                fk_red[c] = stamp("vector", vE, lambda e: e.tensor_tensor(
                    Fv(o_kmax, 1), Fv(o_kmax, 1), Fv(o_st2 + 96, 1), op=ALU.max))

        def frontB(c):
            nonlocal sev_prev
            bsel = c % 2
            par = c % 2
            w("vector", tE, xnt_tv[0])
            lnv = ln_norm(o_xst + bsel * 2048, o_xnn, o_st1, ln_se[0])
            ln_done[c] = lnv
            tap_store("xn", B(o_xnn, 1024), c * 128)
            if c + 2 < NCH:
                w("sync", vE, lnv)
                x_dma[c + 2] = dma("sync", dX, Fv(o_xst + (c % 2) * 2048, 1024),
                                   x_d[(c + 2) * 128:(c + 3) * 128, :])
            w("tensor", vE, lnv)
            tg, _ = transpose_group([B(o_xnn + i * 128, 128) for i in range(8)], B(xnt_of(par), 1024))
            xnt_tv[0] = tg
            w("tensor", sE, sev_prev)
            round1(o_w[0], o_sqn, o_pl[0], xnt_of(par), also_g=True)
            tap_store("sq", B(o_sqn, 1024), c * 128)
            gv = stamp("vector", vE, lambda e: e.tensor_tensor(
                Fv(o_st2, H), psB[:, 1536:1536 + H], B(o_plg, H), op=ALU.add))
            w("scalar", vE, gv)
            ge1 = stamp("scalar", sE, lambda e: e.activation(Fv(o_st2 + 32, H), Fv(o_st2, H), AF.Exp))
            w("scalar", sE, ge1)
            spv = stamp("scalar", sE, lambda e: e.activation(Fv(o_sp, H), Fv(o_st2 + 32, H), AF.Ln, bias=cst(1.0)))
            tap_store("sp", Fv(o_sp, H), c * 128)

            w("vector", dG)
            round1(o_w[3], o_vst + bsel * 1024, o_pl[3], xnt_of(par))
            tap_store("v", B(o_vst + bsel * 1024, 1024), c * 128)
            w("gpsimd", vE)
            vdma_prev[bsel] = dma("gpsimd", dG, vext_s[:, c * 1024:(c + 1) * 1024],
                                  B(o_vst + bsel * 1024, 1024))

            transpose_group([B(o_sqn + i * 128, 128) for i in range(8)], B(o_sqT + par * 1024, 1024))

            w("scalar", vE, hv_prev)
            w("tensor", sE, spv)
            mm(psB[:, 1552:1552 + H], Fv(o_btri, 128), Fv(o_sp, H), True, True)
            mm(psB[:, 1568:1568 + H], Fv(o_bones, 128), Fv(o_sp, H), True, True)
            ltv = mml(psB[:, 1584:1584 + H], Fv(o_bones + 256, 128), Fv(o_sp, H), True, True)
            w("scalar", tE, ltv)
            on("scalar", lambda e, o=Fv(o_eml + par * 32, H): e.activation(o, psB[:, 1552:1552 + H], AF.Exp))
            on("scalar", lambda e, o=Fv(o_el + par * 32, H): e.activation(o, psB[:, 1552:1552 + H], AF.Exp, scale=cst(-1.0)))
            on("scalar", lambda e: e.activation(Fv(o_ea, H), psB[:, 1568:1568 + H], AF.Exp, scale=cst(-1.0)))
            sev = stamp("scalar", sE, lambda e: e.activation(Fv(o_ea + 32, H), psB[:, 1584:1584 + H],
                                                             AF.Exp, scale=cst(-1.0)))
            sev_prev = sev
            w("vector", sE, sev)
            for sub in range(2):
                src = o_ea + sub * 32
                even = Fv(src, 16, 0, 64).rearrange("p (a b) -> p a b", b=2)[:, :, 0:1].rearrange("p a b -> p (a b)")
                odd = Fv(src, 16, 64, 128).rearrange("p (a b) -> p a b", b=2)[:, :, 1:2].rearrange("p a b -> p (a b)")
                stamp("vector", vE, lambda e, s=even, d=Fv(o_ep + par * 32 + sub * 16, 8, 0, 64): e.tensor_copy(d, s))
                stamp("vector", vE, lambda e, s=odd, d=Fv(o_ep + par * 32 + sub * 16, 8, 64, 128): e.tensor_copy(d, s))

        def back(c):
            nonlocal m_par, hv_prev, skv
            par = c % 2
            tap_sync("vector")
            round1(o_w[2], o_svn, o_pl[2], xnt_of(par))
            tap_store("sv", B(o_svn, 1024), c * 128)
            round1(o_w[1], o_skn, o_pl[1], xnt_of(par))
            vsync()
            skv = stamp("vector", vE, lambda e, eb=bcast_h(o_eml + par * 32): e.tensor_tensor(
                g3(B(o_sk2, 1024)), g3(B(o_skn, 1024)), eb, op=ALU.mult))
            tap_store("sk2", B(o_sk2, 1024), c * 128)
            w("tensor", vE, skv)
            transpose_group([B(o_sk2 + i * 128, 128) for i in range(8)], B(o_sk2T, 1024))

            w("tensor", vE, hv_prev)
            mv = 0
            for sub in range(2):
                r0 = sub * 64
                for hp_g in range(2):
                    for h in range(hp_g, H, 2):
                        j, hp = h // 2, h % 2
                        bank, ch = (0, h) if h < 8 else (1, h - 8)
                        mm(Bk(bank, r0, r0 + 64, ch * 64, ch * 64 + 64),
                           B(o_sqT + par * 1024 + j * 128 + r0, 64, hp * 64, hp * 64 + 64),
                           B(o_m + m_par * 512 + j * 64, 64, hp * 64, hp * 64 + 64),
                           h in (0, 8), False, tile=(hp * 64, r0))
                    on("tensor", lambda e: e.drain())
                for half in range(2):
                    tv = 0
                    for hp_g in range(2):
                        for hh in range(hp_g, 8, 2):
                            h = half * 8 + hh
                            j, hp = h // 2, h % 2
                            f = mml if hh == 7 else mm
                            tv = f(A(3, r0, r0 + 64, hh * 64, hh * 64 + 64),
                                   B(o_sk2T + j * 128 + r0, 64, hp * 64, hp * 64 + 64),
                                   B(o_sqT + par * 1024 + j * 128 + r0, 64, hp * 64, hp * 64 + 64),
                                   hh == 0, hh == 7, tile=(hp * 64, r0)) or tv
                        on("tensor", lambda e: e.drain())
                    w("vector", tE, tv)
                    mk = arena[r0:r0 + 64, o_mask:o_mask + 64].unsqueeze(1).broadcast_to([64, 8, 64])
                    pv = stamp("vector", vE, lambda e, o=g3(B(o_pp + half * 512, 512, r0, r0 + 64), 8),
                               i=g3(A(3, r0, r0 + 64), 8), m=mk: e.tensor_tensor(o, i, m, op=ALU.mult))
                    w("tensor", vE, pv)
                for h in range(H):
                    half, hh = h // 8, h % 8
                    bank, ch = (0, h) if h < 8 else (1, h - 8)
                    mm(Bk(bank, r0, r0 + 64, ch * 64, ch * 64 + 64),
                       B(o_pp + half * 512 + hh * 64, 64, r0, r0 + 64),
                       B(o_svn + h * 64, 64, r0, r0 + 64),
                       False, h in (7, 15), tile=(r0, r0))
                stv = 0
                for h in range(H):
                    j, hp = h // 2, h % 2
                    f = mml if h == 15 else mm
                    stv = f(Bk(2, hp * 64, hp * 64 + 64, j * 64, j * 64 + 64),
                            B(o_sk2 + h * 64, 64, r0, r0 + 64),
                            B(o_svn + h * 64, 64, r0, r0 + 64),
                            h in (0, 1), h in (14, 15), tile=(r0, hp * 64)) or stv
                w("vector", tE, stv)
                w("vector", vE)
                stamp("vector", vE, lambda e, mp=m_par: e.tensor_tensor(
                    Fv(o_pp, 512), Bk(2), B(o_m + mp * 512, 512), op=ALU.add))
                vsync()
                ep_b = Fv(o_ep + par * 32 + sub * 16, 8).unsqueeze(2).broadcast_to([128, 8, 64])
                mv = stamp("vector", vE, lambda e, np_=1 - m_par, eb=ep_b: e.tensor_tensor(
                    g3(B(o_m + np_ * 512, 512), 8), g3(Fv(o_pp, 512), 8), eb, op=ALU.mult))
                w("tensor", vE, mv)
                m_par = 1 - m_par
            hv_prev = stamp("vector", vE, lambda e, eb=bcast_h(o_el + par * 32): e.tensor_tensor(
                g3(B(o_hn, 1024)), g3(psB[:, 0:1024]), eb, op=ALU.mult))
            tap_store("hseq", B(o_hn, 1024), c * 128)
            w("tensor", vE, hv_prev)
            transpose_group([B(o_hn + i * 128, 128) for i in range(8)], B(o_big + c * 1024, 1024))

        for c in range(NCH + 1):
            if c < NCH:
                frontA(c)
            if c >= 1:
                back(c - 1)
            if c < NCH:
                frontB(c)

        if STOP_PHASE is not None and STOP_PHASE < 2:
            _en[0] = False
        prep_weightT(wd["Wq"], o_w2[0])
        prep_weightT(wd["Wk"], o_w2[1])
        build_plane(wd["bq"], o_pl[0], C)
        build_plane(wd["bk"], o_pl[1], C)
        w("gpsimd", vE)
        with nc.allow_non_contiguous_dma("fold [128,1] maxabs to one row"):
            gmx = dma("gpsimd", dG, Fv(o_small + 128, 128, 0, 1), Fv(o_kmax, 1))
        w("vector", dG)
        rmx = stamp("vector", vE, lambda e: e.tensor_reduce(
            Fv(o_kmax + 2, 1, 0, 1), Fv(o_small + 128, 128, 0, 1), axis=AX.X, op=ALU.max))
        w("tensor", vE, rmx)
        mm(psA[:, 1536:1537], Fv(o_onef, 128, 0, 1), Fv(o_kmax + 2, 1, 0, 1), True, True)
        tvm = tdrain()
        w("vector", tE, tvm)
        stamp("vector", vE, lambda e: e.tensor_copy(Fv(o_kmax + 8, 1), psA[:, 1536:1537]))
        vsync()
        stamp("vector", vE, lambda e: e.reciprocal(Fv(o_kmax + 12, 1), Fv(o_kmax + 8, 1)))
        vsync()
        stamp("vector", vE, lambda e: e.tensor_scalar(Fv(o_kmax + 12, 1), Fv(o_kmax + 12, 1), 7.0, None, op0=ALU.mult))
        vsync()
        fsc = stamp("vector", vE, lambda e: e.tensor_scalar(Fv(o_kmax + 8, 1), Fv(o_kmax + 8, 1),
                                                            1.0 / 7.0, None, op0=ALU.mult))

        qdma_prev = 0
        vload = [0] * NCH
        rnd = [0] * NCH
        krnd = [0] * NCH

        def norm_path(pfull, plane, rawo, rso, outo, fold_cb, tgate):
            tap_sync("vector")
            first = stamp("vector", vE, lambda e, pf=pfull, pl=B(plane, 1024):
                          e.tensor_tensor(B(rawo, 1024), pf, pl, op=ALU.add))
            vsync()
            stamp("vector", vE, lambda e: e.tensor_tensor(Fv(o_sk2, 1024), B(rawo, 1024), B(rawo, 1024), op=ALU.mult))
            vsync()
            rv = stamp("vector", vE, lambda e: e.tensor_reduce(
                Fv(rso, H), Fv(o_sk2, 1024).rearrange("p (a b) -> p a b", a=H), axis=AX.X, op=ALU.add))
            w("scalar", vE, rv)
            nl = stamp("scalar", sE, lambda e: e.activation(Fv(rso + 32, H), Fv(rso, H), AF.Ln, bias=cst(1e-19)))
            w("scalar", sE, nl)
            sv2 = stamp("scalar", sE, lambda e: e.activation(Fv(rso, H), Fv(rso + 32, H), AF.Exp, scale=cst(-0.5)))
            w("vector", sE, sv2)
            if fold_cb:
                stamp("vector", vE, lambda e: e.tensor_tensor(Fv(rso, H), Fv(rso, H), Fv(o_cb, H), op=ALU.mult))
                vsync()
            w("vector", tE, tgate)
            nv = stamp("vector", vE, lambda e: e.tensor_tensor(
                g3(B(outo, 1024)), g3(B(rawo, 1024)), bcast_h(rso), op=ALU.mult))
            return first, nv

        frdK = frdQ = 0
        nk = nq = 0
        tkv = tqv2 = 0
        for c in range(NCH + 1):
            if c >= 1:
                c1 = c - 1
                w("vector", tE, krnd[c1])
                frdK, nk = norm_path(psA[:, 0:1024], o_pl[1], o_sqn, o_rstd, o_hn, True, tkv)
                w("vector", tE, rnd[c1])
                frdQ, nq = norm_path(psB[:, 0:1024], o_pl[0], o_skn, o_rstd + 64, o_hn2, False, tqv2)
            if c < NCH:
                w("tensor", vE, frdK)
                kv = 0
                for kc in range(8):
                    for hf in range(2):
                        f = mml if (kc == 7 and hf == 1) else mm
                        kv = f(A(hf), B(o_big + c * 1024 + kc * 128, 128),
                               B(o_w[1] + kc * 1024 + hf * 512, 512), kc == 0, kc == 7) or kv
                krnd[c] = kv
                w("tensor", vE, frdQ)
                tv = 0
                for kc in range(8):
                    for hf in range(2):
                        last = (kc == 7 and hf == 1)
                        f = mml if last else mm
                        tv = f(Bk(hf), B(o_big + c * 1024 + kc * 128, 128),
                               B(o_w[0] + kc * 1024 + hf * 512, 512), kc == 0, kc == 7) or tv
                rnd[c] = tv
            if c >= 1:
                c1 = c - 1
                w("tensor", vE, nk)
                tap_store("kc", B(o_hn, 1024), c1 * 128)
                tkv, _ = transpose_group([B(o_hn + i * 128, 128) for i in range(8)], B(o_kt + c1 * 1024, 1024))
                w("tensor", vE, nq)
                tap_store("qn", B(o_hn2, 1024), c1 * 128)
                w("vector", dG, qdma_prev)
                tqv2, _ = transpose_group([B(o_hn2 + i * 128, 128) for i in range(8)], B(o_sqT, 1024))
                tb_, tof = c1 // 4, (c1 % 4) * 128
                qdst = qt_s[:, tb_ * 4096:(tb_ + 1) * 4096].rearrange("p (a b) -> p a b", b=512)[:, :, tof:tof + 128]
                w("gpsimd", vE)
                qdma_prev = dma("gpsimd", dG, qdst, B(o_sqT, 1024).rearrange("p (a b) -> p a b", b=128))
                w("sync", tE, rnd[c1])
                w("sync", dG)
                vload[c1] = dma("sync", dS, arena[:, o_big + c1 * 1024:o_big + (c1 + 1) * 1024],
                                vext_s[:, c1 * 1024:(c1 + 1) * 1024])

        if STOP_PHASE is not None and STOP_PHASE < 3:
            _en[0] = False
        prep_weightT(wd["Wo"], o_w2[0], stg=o_p3, stg2=o_rv)
        build_plane(wd["bo"], o_pl3, C, brow=o_p3)
        for eng in Prog.ENGINES:
            w(eng, tE)
            w(eng, vE)
            w(eng, sE)
            w(eng, dG)
            w(eng, dS)
            w(eng, dX)
            w(eng, dF)
            w(eng, dT)
        dma("sync", dS, Fv(o_ind, 256), ind_d)
        w("tensor", dS)
        if "vx" in tap_d:
            dma("gpsimd", dT, tap_d["vx"], B(o_big, T * 8))
        on("vector", lambda e: e.memset(Fv(o_rv, 1024), 0.0))
        rvz = stamp("vector", vE, lambda e: e.memset(B(o_aot, 4096), 0.0))
        w("tensor", vE, rvz)

        qld = [0] * NTB
        qsem = [dX, dF]
        for tb in range(min(2, NTB)):
            qld[tb] = dma("sync", qsem[tb % 2], B(o_qtb + (tb % 2) * 4096, 4096), qt_s[:, tb * 4096:(tb + 1) * 4096])
        prev_exp_par = [0, 0]
        pp_attnv = [0, 0, 0, 0]
        adma_prev = [0, 0]
        st3 = {"aot_par": [0, 0], "rec_par": [0, 0], "bk3": 0, "aot_last": 0}

        def emit_wo_piece(tb, piece):
            t1, hf = piece >> 1, piece & 1
            w("tensor", vE, st3["bk3"])
            if piece == 0:
                w("tensor", vE, st3["aot_last"])
            wov = 0
            for j in range(8):
                f = mml if j == 7 else mm
                wov = f(Bk(3), B(o_aot + j * 512 + t1 * 128, 128),
                        B(o_w2[0] + j * 1024 + hf * 512, 512), j == 0, j == 7) or wov
            w("vector", tE, wov)
            if hf == 0:
                w("vector", dG, adma_prev[t1 % 2])
            ev = stamp("vector", vE, lambda e, o=B(o_att + (t1 % 2) * 1024 + hf * 512, 512),
                       i=Bk(3), pl=B(o_pl3 + hf * 512, 512): e.tensor_tensor(o, i, pl, op=ALU.add))
            st3["bk3"] = ev
            if hf == 1:
                tap_sync("vector")
                tap_store("attn", B(o_att + (t1 % 2) * 1024, 1024), tb * 512 + t1 * 128)
                w("gpsimd", vE, ev)
                w("gpsimd", dG)
                adma_prev[t1 % 2] = dma("gpsimd", dG,
                                        attn_s[tb * 512 + t1 * 128:tb * 512 + (t1 + 1) * 128, :],
                                        B(o_att + (t1 % 2) * 1024, 1024))

        def emit_wo(tb):
            for piece in range(8):
                emit_wo_piece(tb, piece)

        def emit_epi(g, av_last):
            par = g & 1
            abk = 0 if par == 0 else 2
            pr = g & 7
            w("tensor", vE, st3["rec_par"][par])
            w("tensor", vE, st3["bk3"])
            bc = mml(Bk(3), Fv(o_ind + par * 256, 128), Fv(o_rv + par * 1024, 512), True, True)
            w("vector", tE, bc)
            w("vector", vE, st3["aot_last"])
            tap_sync("vector")
            bcv = stamp("vector", vE, lambda e, o=Fv(o_bcv, 512), i=Bk(3): e.tensor_copy(o, i))
            st3["bk3"] = bcv
            w("vector", vE, bcv)
            aot_ev = stamp("vector", vE, lambda e, o=B(o_aot + pr * 512, 512),
                           i0=Bk(abk), i1=Fv(o_bcv, 512): e.tensor_tensor(o, i0, i1, op=ALU.mult))
            st3["aot_par"][par] = aot_ev
            st3["aot_last"] = aot_ev

        av_prev = 0
        NG = NTB * 8
        hoist_scv = [0, 0]

        def emit_scores(g2, s2):
            tb2, pr2 = g2 >> 3, g2 & 7
            qb2 = (tb2 % 2) * 4096
            sb2 = s2 % 2
            w("tensor", sE, prev_exp_par[sb2])
            scv = 0
            for hp in range(2):
                f = mml if hp == 1 else mm
                scv = f(psA[:, sb2 * 1024 + hp * 512:sb2 * 1024 + hp * 512 + 512],
                        B(o_kt + s2 * 1024 + pr2 * 128, 128, hp * 64, hp * 64 + 64),
                        B(o_qtb + qb2 + pr2 * 512, 512, hp * 64, hp * 64 + 64),
                        True, True, tile=(hp * 64, 0)) or scv
            return scv

        for g in range(NG):
            tb, pr = g >> 3, g & 7
            par = g & 1
            abk = 0 if par == 0 else 2
            rbase = par * 32
            if pr == 0 and g == 0:
                w("tensor", qsem[tb % 2], qld[tb])
            tap_sync("scalar")
            tap_sync("vector")
            bk0_gate = st3["aot_par"][par]
            bk1_gate = st3["rec_par"][par]
            exp_hist = [0, 0, 0, 0]
            for s in range(NST + 2):
                if s < NST:
                    sb = s % 2
                    if g > 0 and s < 2:
                        scv = hoist_scv[s]
                    else:
                        scv = emit_scores(g, s)
                    w("scalar", tE, scv)
                    w("scalar", tE, pp_attnv[s % 4])
                    exp_s = stamp("scalar", sE, lambda e, o=B(o_p3 + (s % 4) * 1024, 1024),
                                  i=psA[:, sb * 1024:sb * 1024 + 1024]: e.activation(o, i, AF.Exp))
                    prev_exp_par[sb] = exp_s
                    exp_hist[s % 4] = exp_s
                if s >= 2:
                    s1 = s - 2
                    sb1 = s1 % 4
                    w("tensor", sE, exp_hist[sb1])
                    if s1 == 0:
                        w("tensor", vE, bk0_gate)
                    for hp in range(2):
                        h = pr * 2 + hp
                        mm(Bk(abk, hp * 64, hp * 64 + 64),
                           arena[:, o_big + s1 * 1024 + h * 64:o_big + s1 * 1024 + h * 64 + 64],
                           B(o_p3 + sb1 * 1024 + hp * 512, 512),
                           s1 == 0, s1 == NST - 1, tile=(0, hp * 64))
                    if s1 == 0:
                        w("tensor", vE, bk1_gate)
                    for hp in range(2):
                        f = mml if hp == 1 else mm
                        pp_attnv[sb1] = f(Bk(1, rbase + hp * 64, rbase + hp * 64 + 1),
                                          arena[:, o_onec:o_onec + 1],
                                          B(o_p3 + sb1 * 1024 + hp * 512, 512),
                                          s1 == 0, s1 == NST - 1,
                                          tile=(0, rbase + hp * 64)) or pp_attnv[sb1]
                if pr == 1 and tb >= 1 and 0 < s <= NST and (s * 8) % NST == 0:
                    emit_wo_piece(tb - 1, s * 8 // NST - 1)
                if s == max(2, NST // 2) and g >= 1 and not (pr == 1 and tb >= 1):
                    emit_epi(g - 1, av_prev)
                if s == NST - 1 and g + 1 < NG:
                    if (g + 1) & 7 == 0:
                        w("tensor", qsem[((g + 1) >> 3) % 2], qld[(g + 1) >> 3])
                    for s2 in (0, 1):
                        hoist_scv[s2] = emit_scores(g + 1, s2)
            av_last = pp_attnv[(NST - 1) % 4]
            if g >= 1 and pr == 1 and tb >= 1:
                emit_epi(g - 1, av_prev)
            w("vector", tE, av_last)
            rc = 0
            for rr in (rbase, rbase + 64):
                rc = stamp("vector", vE, lambda e, i=Bk(1, rr, rr + 1),
                           o=Fv(o_rv + par * 1024, 512, rr, rr + 1): e.reciprocal(o, i))
            if pr == 3 and tb + 2 < NTB:
                w("sync", tE)
                qld[tb + 2] = dma("sync", qsem[tb % 2], B(o_qtb + (tb % 2) * 4096, 4096),
                                  qt_s[:, (tb + 2) * 4096:(tb + 3) * 4096])
            av_prev = av_last
            st3["rec_par"][par] = rc
        emit_epi(NTB * 8 - 1, av_prev)
        emit_wo(NTB - 1)

        if STOP_PHASE is not None and STOP_PHASE < 4:
            _en[0] = False
        for eng in Prog.ENGINES:
            w(eng, tE)
            w(eng, vE)
            w(eng, sE)
            w(eng, dG)
            w(eng, dS)
            w(eng, dX)
            w(eng, dF)
            w(eng, dT)
        dva = dma("sync", dS, Fv(o_fst, 128).rearrange("p (a b) -> p a b", b=16),
                  wd["fa"].rearrange("(a p) r -> p a r", p=128))
        dvb = dma("sync", dS, Fv(o_fst + 2048, 1024, 0, 16), wd["fb"])
        w("vector", dS, dvb)
        on("vector", lambda e: e.tensor_copy(B(o_xn2, 128), Fv(o_fst, 128)))
        cvb = stamp("vector", vE, lambda e: e.tensor_copy(B(o_fbB, 1024, 0, 16), Fv(o_fst + 2048, 1024, 0, 16)))
        w("tensor", vE, cvb)
        for i in range(8):
            on("tensor", lambda e, s=B(o_xn2 + i * 16, 16), o=A2bf(i * 128, (i + 1) * 128, 0, 16),
               f=(i == 0), l=(i == 7), ид=arena[:, o_id:o_id + 128]:
               e.matmul(o, s, ид, is_transpose=True, start=f, stop=l, skip_group_check=True))
        tva = tdrain()
        w("vector", tE, tva)
        ave = stamp("vector", vE, lambda e: e.tensor_copy(B(o_aT, 1024, 0, 16), A2bf(0, 1024, 0, 16)))
        w("tensor", vE, ave)
        build_plane(wd["fbias"], o_pl4, C, brow=o_x4a)
        for kc in range(8):
            bsel = kc % 2
            w("sync", vE)
            dv = dma("sync", dS, Fv(o_fst + bsel * 2048, 1024), wd["fk"][kc * 128:(kc + 1) * 128, :])
            w("vector", dS, dv)
            w("vector", vE, fsc)
            w("vector", vE)
            stamp("vector", vE, lambda e, i=Fv(o_fst + bsel * 2048, 1024): e.tensor_scalar(
                Fv(o_wq4, 1024), i, Fv(o_kmax + 12, 1), MAGIC, op0=ALU.mult, op1=ALU.add))
            vsync()
            stamp("vector", vE, lambda e: e.tensor_scalar(
                Fv(o_wq4, 1024), Fv(o_wq4, 1024), -MAGIC, Fv(o_kmax + 8, 1), op0=ALU.add, op1=ALU.mult))
            tv = 0
            for hf in range(2):
                f = mml if hf == 1 else mm
                tv = f(A(hf), B(o_aT + kc * 128, 128, 0, 16), B(o_fbB + hf * 512, 512, 0, 16), True, True) or tv
            w("vector", tE, tv)
            vsync()
            wfv = stamp("vector", vE, lambda e, d=B(o_w2[1] + kc * 1024, 1024):
                        e.tensor_tensor(d, psA[:, 0:1024], Fv(o_wq4, 1024), op=ALU.add))
            w("tensor", vE, wfv)
        x4a = [0] * NCH
        x4b = [0] * NCH
        ln4 = [0] * NCH
        ln4se = [0] * NCH
        w("sync", tE)
        w("sync", vE)
        for c in range(min(2, NCH)):
            x4a[c] = dma("sync", dX, Fv(o_x4a + (c % 2) * 2048, 1024), x_d[c * 128:(c + 1) * 128, :])
            x4b[c] = dma("sync", dF, B(o_x4b + (c % 2) * 1024, 1024), attn_s[c * 128:(c + 1) * 128, :])
        odma_prev = [0, 0]
        out_dma = 0
        yfree = 0
        xnT_gate = 0
        w("vector", tE, tva)
        for c in range(NCH + 1):
            if c < NCH:
                bsel = c % 2
                tap_sync("vector")
                tap_sync("sync")
                w("vector", dX)
                w("vector", dF)
                x1v = stamp("vector", vE, lambda e, a=Fv(o_x4a + bsel * 2048, 1024),
                            b=B(o_x4b + bsel * 1024, 1024): e.tensor_tensor(a, a, b, op=ALU.add))
                tap_store("x1", Fv(o_x4a + bsel * 2048, 1024), c * 128)
                vsync()
                ln4se[c] = ln_stats(o_x4a + bsel * 2048, o_s4 + bsel * 32, o_lnsq4)
            if c >= 1:
                c1 = c - 1
                b1 = c1 % 2
                ln4[c1] = ln_norm(o_x4a + b1 * 2048, o_xn2 + b1 * 1024, o_s4 + b1 * 32, ln4se[c1])
                w("tensor", vE, ln4[c1])
                w("vector", tE, xnT_gate)
                transpose_group([B(o_xn2 + b1 * 1024 + i * 128, 128) for i in range(8)], B(o_x2T, 1024))
                w("tensor", vE, yfree)
                ytv = 0
                for kc in range(8):
                    for hf in range(2):
                        last = (kc == 7 and hf == 1)
                        f = mml if last else mm
                        ytv = f(A(hf), B(o_x2T + kc * 128, 128), B(o_w2[1] + kc * 1024 + hf * 512, 512),
                                kc == 0, kc == 7) or ytv
                xnT_gate = ytv
                w("vector", tE, ytv)
                w("vector", dG, odma_prev[b1])
                yfree = stamp("vector", vE, lambda e, o=Fv(o_ou + b1 * 2048, 1024): e.tensor_tensor(
                    o, psA[:, 0:1024], B(o_pl4, 1024), op=ALU.add))
                vsync()
                ov_ = stamp("vector", vE, lambda e, o=Fv(o_ou + b1 * 2048, 1024),
                            x1=Fv(o_x4a + b1 * 2048, 1024): e.tensor_tensor(o, o, x1, op=ALU.add))
                w("gpsimd", vE, ov_)
                w("gpsimd", dG)
                out_dma = dma("gpsimd", dG, out_d[c1 * 128:(c1 + 1) * 128, :], Fv(o_ou + b1 * 2048, 1024))
                odma_prev[b1] = out_dma
                if c1 + 2 < NCH:
                    tap_sync("sync")
                    w("sync", vE, ov_)
                    x4a[c1 + 2] = dma("sync", dX, Fv(o_x4a + b1 * 2048, 1024),
                                      x_d[(c1 + 2) * 128:(c1 + 3) * 128, :])
                    x4b[c1 + 2] = dma("sync", dF, B(o_x4b + b1 * 1024, 1024),
                                      attn_s[(c1 + 2) * 128:(c1 + 3) * 128, :])
        w("gpsimd", dG, out_dma)

        blk = ctx.enter_context(nc.Block())
        p.run(blk)
    return nc


def make_consts():
    ident = np.eye(128, dtype=ml_dtypes.bfloat16)
    u = np.arange(128)
    btri = ((u[:, None] // 64 == u[None, :] // 64) & (u[:, None] <= u[None, :])).astype(np.float32)
    bones = np.zeros((128, 256), np.float32)
    bones[0:64, 0:128] = 1.0
    bones[64:128, 128:256] = 1.0
    m = (np.arange(64)[:, None] <= np.arange(64)[None, :]).astype(np.float32)
    mask2 = np.concatenate([m, m], 0).astype(ml_dtypes.bfloat16)
    ind = np.zeros((128, 256), np.float32)
    ind[0, 0:64] = 1.0
    ind[64, 64:128] = 1.0
    ind[32, 128:192] = 1.0
    ind[96, 192:256] = 1.0
    identf = np.eye(128, dtype=np.float32)
    return {"ident": ident, "btri": btri, "bones": bones, "mask2": mask2, "ind": ind,
            "identf": identf}


def make_in_map(xb, inputs):
    m = {"x": np.ascontiguousarray(np.asarray(xb, np.float32))}
    ren = {"Wsq": "L0_attn_Wsq", "bsq": "L0_attn_bsq", "Wsk": "L0_attn_Wsk", "bsk": "L0_attn_bsk",
           "Wsv": "L0_attn_Wsv", "bsv": "L0_attn_bsv", "Wg": "L0_attn_Wg", "bg": "L0_attn_bg",
           "Wq": "L0_attn_Wq", "bq": "L0_attn_bq", "Wk": "L0_attn_Wk", "bk": "L0_attn_bk",
           "Wv": "L0_attn_Wv", "bv": "L0_attn_bv", "Wo": "L0_attn_Wo", "bo": "L0_attn_bo",
           "temp": "L0_attn_temp", "fk": "L0_ffn_kernel", "fbias": "L0_ffn_bias",
           "fa": "L0_ffn_a", "fb": "L0_ffn_b"}
    for k, src in ren.items():
        v = np.asarray(inputs[src], np.float32)
        if k in ("bsq", "bsk", "bsv", "bq", "bk", "bv", "bo", "fbias", "bg", "temp"):
            v = v.reshape(1, -1)
        m[k] = np.ascontiguousarray(v)
    m.update(make_consts())
    return m


def kernel(**inputs):
    x = np.asarray(inputs["L0_attn_x"], np.float32)
    Bn, T, _ = x.shape
    nc = build(T=T)
    in_maps = [make_in_map(x[b], inputs) for b in range(Bn)]
    res = run_bass_kernel_spmd(nc, in_maps, core_ids=list(range(Bn)))
    return np.stack([np.asarray(r["out"], np.float32) for r in res.results], 0)


# version 28
# speedup vs baseline: 1.0172x; 1.0109x over previous
import contextlib
import numpy as np
import ml_dtypes

import concourse.bass as bass
import concourse.mybir as mybir
import bass_rust
from concourse import library_config
from concourse.bass_utils import run_bass_kernel_spmd

F32 = mybir.dt.float32
BF16 = mybir.dt.bfloat16
AF = mybir.ActivationFunctionType
ALU = mybir.AluOpType
AX = mybir.AxisListType

C = 1024
H = 16
MAGIC = float(np.float32(3 * 2 ** 22))


class Ctr:
    def __init__(self, h):
        self.h = h
        self.v = 0


class Prog:
    ENGINES = ("sync", "gpsimd", "vector", "scalar", "tensor")

    def __init__(self):
        self.q = {e: [] for e in self.ENGINES}

    def run(self, block):
        for eng in self.ENGINES:
            fns = self.q[eng]
            getattr(block, eng)(lambda e, fns=fns: [f(e) for f in fns])


STOP_PHASE = None


def build(T=4096, taps=()):
    nc = bass.Bass("TRN2", target_bir_lowering=False)
    NCH = T // 128
    NTB = T // 512
    NST = T // 128

    x_d = nc.dram_tensor("x", [T, C], F32, kind="ExternalInput").ap()
    wd = {}
    for nm in ("Wsq", "Wsk", "Wsv", "Wq", "Wk", "Wv", "Wo", "fk"):
        wd[nm] = nc.dram_tensor(nm, [C, C], F32, kind="ExternalInput").ap()
    for nm in ("bsq", "bsk", "bsv", "bq", "bk", "bv", "bo", "fbias"):
        wd[nm] = nc.dram_tensor(nm, [1, C], F32, kind="ExternalInput").ap()
    wd["Wg"] = nc.dram_tensor("Wg", [H, C], F32, kind="ExternalInput").ap()
    wd["bg"] = nc.dram_tensor("bg", [1, H], F32, kind="ExternalInput").ap()
    wd["temp"] = nc.dram_tensor("temp", [1, H], F32, kind="ExternalInput").ap()
    wd["fa"] = nc.dram_tensor("fa", [C, 16], F32, kind="ExternalInput").ap()
    wd["fb"] = nc.dram_tensor("fb", [16, C], F32, kind="ExternalInput").ap()
    ident_d = nc.dram_tensor("ident", [128, 128], BF16, kind="ExternalInput").ap()
    btri_d = nc.dram_tensor("btri", [128, 128], F32, kind="ExternalInput").ap()
    bones_d = nc.dram_tensor("bones", [128, 256], F32, kind="ExternalInput").ap()
    mask2_d = nc.dram_tensor("mask2", [128, 64], BF16, kind="ExternalInput").ap()
    ind_d = nc.dram_tensor("ind", [128, 256], F32, kind="ExternalInput").ap()
    idf_d = nc.dram_tensor("identf", [128, 128], F32, kind="ExternalInput").ap()
    out_d = nc.dram_tensor("out", [T, C], F32, kind="ExternalOutput").ap()

    vext_s = nc.dram_tensor("vext_s", [128, T * 8], BF16, kind="Internal").ap()
    qt_s = nc.dram_tensor("qt_s", [128, T * 8], BF16, kind="Internal").ap()
    attn_s = nc.dram_tensor("attn_s", [T, C], BF16, kind="Internal").ap()

    tshape = {"xn": [T, C], "sq": [T, C], "sk2": [T, C], "sv": [T, C], "v": [T, C],
              "sp": [T, H], "hseq": [T, C], "kc": [T, C], "qn": [T, C],
              "attn": [T, C], "x1": [T, C], "p3": [128, 2048], "aot": [128, 4096],
              "rv": [128, 1024], "vx": [128, T * 8], "ovp": [128, 2048], "wot": [128, 8192], "wop": [128, 2048]}
    tap_d = {t_: nc.dram_tensor("tap_" + t_, tshape[t_], BF16 if t_ in ("attn",) else F32,
                                kind="ExternalOutput").ap()
             for t_ in taps}

    p = Prog()
    _en = [True]

    def on(eng, fn):
        if _en[0]:
            p.q[eng].append(fn)

    ctx = contextlib.ExitStack()
    with ctx:
        AR = 98304
        arena = ctx.enter_context(nc.sbuf_tensor("arena", [128, AR], BF16))

        def B(off, n, p0=0, p1=128):
            return arena[p0:p1, off:off + n]

        def Fv(off, n, p0=0, p1=128):
            assert off % 2 == 0
            return arena[p0:p1, off:off + 2 * n].bitcast(F32)

        def cst(v, p0=0, p1=128):
            return Fv(o_cst + 2 * CSTV[v], 1, p0, p1)

        class Alloc:
            def __init__(self, base):
                self.o = base

            def __call__(self, n):
                self.o = (self.o + 1) // 2 * 2
                r = self.o
                self.o += n
                assert self.o <= AR, self.o
                return r

        g_al = Alloc(0)
        o_big = g_al(T * 8)
        KTR = max(T * 8, 32768)
        o_ktr = g_al(KTR)
        o_w2 = [g_al(8192), g_al(8192)]
        o_id = g_al(128)
        o_cb = g_al(32)
        o_onec = g_al(2)
        o_oner = g_al(128)
        o_kmax = g_al(16)
        o_cst = g_al(32)
        o_spc = g_al(4)
        o_mask = g_al(64)
        o_idf = g_al(256)
        o_onef = g_al(256)
        OVB = g_al.o
        CSTV = {1e-5: 0, 1.0: 1, 1e-19: 2, -0.5: 3, -1.0: 4}

        o_kt = o_ktr
        k1 = Alloc(o_ktr)
        o_wsv = k1(8192)
        o_wv = k1(8192)
        o_xst = k1(2 * 2048)
        o_wst = k1(2 * 2048)
        o_lnsq = k1(2048)
        o_vst = k1(2 * 1024)
        o_m = k1(2 * 512)
        o_pp = k1(1024)
        o_xnn = k1(1024)
        o_xnT = k1(1024)
        assert k1.o <= o_ktr + KTR, k1.o
        o_w = [o_w2[0], o_w2[1], o_wsv, o_wv]

        a1 = Alloc(OVB)
        o_pl = [a1(1024) for _ in range(4)]
        o_plg = a1(16)
        o_btri = a1(256)
        o_bones = a1(512)
        o_wg = a1(128)
        o_small = a1(256)
        o_sqn = a1(1024)
        o_skn = a1(1024)
        o_sk2 = a1(1024)
        o_svn = a1(1024)
        o_sqT = a1(2 * 1024)
        o_sk2T = a1(1024)
        o_hn = a1(1024)
        o_hn2 = a1(1024)
        o_xnT2 = o_hn2
        o_sp = a1(32)
        o_eml = a1(64)
        o_el = a1(64)
        o_ea = a1(64)
        o_ep = a1(64)
        o_st1 = a1(64)
        o_st2 = a1(128)
        o_rstd = a1(128)
        o_brow = o_xst
        p1_end = a1.o

        a3 = Alloc(OVB)
        o_p3 = a3(2 * 2048)
        o_aot = a3(4096)
        o_rv = a3(2048)
        o_bcv = a3(1024)
        o_att = a3(2 * 1024)
        o_ind = a3(512)
        o_pl3 = a3(1024)
        o_qtb = o_w2[1]
        p3_end = a3.o

        a4 = Alloc(o_ktr)
        o_x4a = a4(2 * 2048)
        o_x4b = a4(2 * 1024)
        o_fst = a4(2 * 2048)
        o_wq4 = a4(2048)
        o_xn2 = a4(2 * 1024)
        o_x2T = a4(1024)
        o_aT = a4(1024)
        o_fbB = a4(1024)
        o_ou = a4(2 * 2048)
        o_s4 = a4(64)
        o_lnsq4 = a4(2048)
        o_pl4 = a4(1024)
        assert a4.o <= o_ktr + KTR, a4.o
        assert max(p1_end, p3_end) <= AR, (p1_end, p3_end, AR)

        psA = ctx.enter_context(nc.psum_tensor("psA", [128, 2048], F32))
        psB = ctx.enter_context(nc.psum_tensor("psB", [128, 2048], F32))
        psA_bf = psA.ap().bitcast(BF16)

        def A(i, p0=0, p1=128, c0=0, c1=512):
            return psA[p0:p1, i * 512 + c0:i * 512 + c1]

        def Bk(i, p0=0, p1=128, c0=0, c1=512):
            return psB[p0:p1, i * 512 + c0:i * 512 + c1]

        def A2bf(c0, c1, p0=0, p1=128):
            return psA_bf[p0:p1, 2048 + c0:2048 + c1]

        dS = Ctr(ctx.enter_context(nc.semaphore("dS")))
        dG = Ctr(ctx.enter_context(nc.semaphore("dG")))
        tE = Ctr(ctx.enter_context(nc.semaphore("tE")))
        vE = Ctr(ctx.enter_context(nc.semaphore("vE")))
        sE = Ctr(ctx.enter_context(nc.semaphore("sE")))
        gC = Ctr(ctx.enter_context(nc.semaphore("gC")))
        dX = Ctr(ctx.enter_context(nc.semaphore("dX")))
        dT = Ctr(ctx.enter_context(nc.semaphore("dT")))
        dF = Ctr(ctx.enter_context(nc.semaphore("dF")))

        def w(eng, ctr, v=None):
            v = ctr.v if v is None else v
            if v > 0:
                on(eng, lambda e, h=ctr.h, v=v: e.wait_ge(h, v))

        def stamp(eng, ctr, fn):
            ctr.v += 1
            on(eng, lambda e, fn=fn, h=ctr.h: fn(e).then_inc(h, 1))
            return ctr.v

        def dma(eng, ctr, out, in_):
            ctr.v += 16
            on(eng, lambda e, o=out, i=in_, h=ctr.h: e.dma_start(out=o, in_=i).then_inc(h, 16))
            return ctr.v

        def vsync():
            w("vector", vE)

        def mm(out, lhsT, rhs, start, stop, tile=None):
            on("tensor", lambda e, o=out, l=lhsT, r=rhs, s=start, st=stop, tp=tile:
               e.matmul(o, l, r, start=s, stop=st, tile_position=tp, skip_group_check=True))

        def tdrain():
            return stamp("tensor", tE, lambda e: e.drain(fusable=False))

        def mml(out, lhsT, rhs, start, stop, tile=None):
            tE.v += 1
            on("tensor", lambda e, o=out, l=lhsT, r=rhs, st_=start, sp=stop, tp=tile, h=tE.h:
               e.matmul(o, l, r, start=st_, stop=sp, tile_position=tp, skip_group_check=True).then_inc(h, 1))
            return tE.v

        def bcast_h(off_f32, g=H, w_=64, p0=0, p1=128):
            return Fv(off_f32, g, p0, p1).unsqueeze(2).broadcast_to([p1 - p0, g, w_])

        def g3(ap, g=H):
            return ap.rearrange("p (a b) -> p a b", a=g)

        def transpose_group(srcs, dst, tw=128):
            n = len(srcs)
            for i, s in enumerate(srcs):
                if i < n - 1:
                    on("tensor", lambda e, s=s, o=A2bf(i * tw, (i + 1) * tw), f=(i == 0),
                       ид=arena[:, o_id:o_id + 128]: e.matmul(o, s, ид, is_transpose=True, start=f, stop=False, skip_group_check=True))
                else:
                    tE.v += 1
                    on("tensor", lambda e, s=s, o=A2bf(i * tw, (i + 1) * tw), f=(i == 0), h=tE.h:
                       e.matmul(o, s, arena[:, o_id:o_id + 128], is_transpose=True, start=f, stop=True,
                                skip_group_check=True).then_inc(h, 1))
            tv = tE.v
            w("vector", tE, tv)
            vE.v += 1
            src = A2bf(0, n * tw)
            if dst.ndim == 3:
                src = src.rearrange("p (a b) -> p a b", b=tw)
            on("vector", lambda e, d=dst, src=src, h=vE.h:
               e.tensor_copy(d, src).then_inc(h, 1))
            ev = vE.v
            w("tensor", vE, ev)
            return tv, ev

        def ln_chunk(xo, outo, so, sqo):
            se = ln_stats(xo, so, sqo)
            return ln_norm(xo, outo, so, se)

        def tap_store(name, ap, row0, n=128):
            if name in tap_d:
                w("gpsimd", vE)
                w("gpsimd", sE)
                w("gpsimd", dT)
                dma("gpsimd", dT, tap_d[name][row0:row0 + n, :], ap)

        def tap_sync(eng):
            if tap_d:
                w(eng, dT)

        def build_plane(bias_ap, plane_off, width, brow=None):
            brow = o_brow if brow is None else brow
            tap_sync("sync")
            tap_sync("vector")
            w("sync", vE)
            dv = dma("sync", dS, Fv(brow, width, 0, 1), bias_ap)
            w("vector", dS, dv)
            cv = stamp("vector", vE, lambda e, w_=width, br=brow:
                       e.tensor_copy(B(br + 2048, w_, 0, 1), Fv(br, w_, 0, 1)))
            w("tensor", vE, cv)
            nh = (width + 511) // 512
            for i in range(nh):
                wdt = min(512, width - i * 512)
                mm(psA[:, 1024 + i * 512:1024 + i * 512 + wdt],
                   arena[0:1, o_oner:o_oner + 128],
                   B(brow + 2048 + i * 512, wdt, 0, 1), True, True)
            tv = tdrain()
            w("vector", tE, tv)
            ev = stamp("vector", vE, lambda e, po=plane_off, w_=width:
                       e.tensor_copy(B(po, w_), psA[:, 1024:1024 + w_]))
            w("tensor", vE, ev)
            return ev

        def prep_weightT(w_ap, slot_off, stg=None, stg2=None):
            stg = o_wst if stg is None else stg
            stg2 = o_sqn if stg2 is None else stg2
            tap_sync("vector")
            tap_sync("sync")
            for kc in range(8):
                bsel = kc % 2
                w("sync", vE)
                dv = dma("sync", dS, Fv(stg + bsel * 2048, 1024), w_ap[kc * 128:(kc + 1) * 128, :])
                w("vector", dS, dv)
                cv = stamp("vector", vE, lambda e, b=bsel, g_=stg, g2=stg2:
                           e.tensor_copy(B(g2, 1024), Fv(g_ + b * 2048, 1024)))
                w("tensor", vE, cv)
                dst = B(slot_off, 8192).rearrange("p (a b) -> p a b", b=1024)[:, :, kc * 128:(kc + 1) * 128]
                transpose_group([B(stg2 + i * 128, 128) for i in range(8)], dst)

        def ln_stats(xo, so, sqo):
            stamp("vector", vE, lambda e, x=Fv(xo, 1024), o=Fv(so, 1): e.reduce_sum(o, x, axis=AX.X))
            stamp("vector", vE, lambda e, x=Fv(xo, 1024), o=Fv(sqo, 1024): e.tensor_tensor(o, x, x, op=ALU.mult))
            vsync()
            stamp("vector", vE, lambda e, x=Fv(sqo, 1024), o=Fv(so + 2, 1): e.reduce_sum(o, x, axis=AX.X))
            stamp("vector", vE, lambda e, i=Fv(so, 1), o=Fv(so + 4, 1): e.tensor_scalar(o, i, 1.0 / C, None, op0=ALU.mult))
            vsync()
            stamp("vector", vE, lambda e, i=Fv(so + 2, 1), o=Fv(so + 6, 1): e.tensor_scalar(o, i, 1.0 / C, None, op0=ALU.mult))
            stamp("vector", vE, lambda e, i=Fv(so + 4, 1), o=Fv(so + 8, 1): e.tensor_tensor(o, i, i, op=ALU.mult))
            vsync()
            sv_ = stamp("vector", vE, lambda e, a=Fv(so + 6, 1), b=Fv(so + 8, 1), o=Fv(so + 10, 1):
                        e.tensor_tensor(o, a, b, op=ALU.subtract))
            w("scalar", vE, sv_)
            sl = stamp("scalar", sE, lambda e, i=Fv(so + 10, 1), o=Fv(so + 12, 1):
                       e.activation(o, i, AF.Ln, bias=cst(1e-5)))
            w("scalar", sE, sl)
            se = stamp("scalar", sE, lambda e, i=Fv(so + 12, 1), o=Fv(so + 14, 1):
                       e.activation(o, i, AF.Exp, scale=cst(-0.5)))
            return se

        def ln_norm(xo, outo, so, se):
            w("vector", sE, se)
            vsync()
            rv = stamp("vector", vE, lambda e, x=Fv(xo, 1024), o=B(outo, 1024), m=Fv(so + 4, 1), r=Fv(so + 14, 1):
                       e.tensor_scalar(o, x, m, r, op0=ALU.subtract, op1=ALU.mult))
            return rv

        dma("sync", dS, arena[:, o_id:o_id + 128], ident_d)
        dma("sync", dS, Fv(o_btri, 128), btri_d)
        dma("sync", dS, Fv(o_bones, 256), bones_d)
        dma("sync", dS, arena[:, o_mask:o_mask + 64], mask2_d)
        dma("sync", dS, Fv(o_idf, 128), idf_d)
        dv_temp = dma("sync", dS, Fv(o_small, H, 0, 1), wd["temp"])
        on("vector", lambda e: e.memset(arena[0:1, o_oner:o_oner + 128], 1.0))
        on("vector", lambda e: e.memset(Fv(o_onef, 128, 0, 1), 1.0))
        on("vector", lambda e: e.memset(arena[:, o_onec:o_onec + 1], 1.0))
        on("vector", lambda e: e.memset(B(o_m, 1024), 0.0))
        for _v, _i in CSTV.items():
            on("vector", lambda e, o=Fv(o_cst + 2 * _i, 1), v=float(_v): e.memset(o, v))
        mz = stamp("vector", vE, lambda e: e.memset(Fv(o_kmax, 8), 0.0))
        w("tensor", dS)
        w("scalar", dS, dv_temp)
        w("scalar", vE, mz)
        pe1 = stamp("scalar", sE, lambda e: e.activation(Fv(o_small + 32, H, 0, 1), Fv(o_small, H, 0, 1), AF.Exp))
        w("scalar", sE, pe1)
        sp_t = stamp("scalar", sE, lambda e: e.activation(Fv(o_small + 64, H, 0, 1), Fv(o_small + 32, H, 0, 1),
                                                          AF.Ln, bias=cst(1.0, 0, 1)))
        w("vector", sE, sp_t)
        w("vector", vE, mz)
        cb_c = stamp("vector", vE, lambda e: e.tensor_copy(B(o_brow + 2048, H, 0, 1), Fv(o_small + 64, H, 0, 1)))
        w("tensor", vE, cb_c)
        mm(psA[:, 1536:1536 + H], arena[0:1, o_oner:o_oner + 128], B(o_brow + 2048, H, 0, 1), True, True)
        tv = tdrain()
        w("vector", tE, tv)
        cb_e = stamp("vector", vE, lambda e: e.tensor_copy(Fv(o_cb, H), psA[:, 1536:1536 + H]))
        w("tensor", vE, cb_e)

        for wnm, slot in (("Wsq", 0), ("Wsk", 1), ("Wsv", 2), ("Wv", 3)):
            prep_weightT(wd[wnm], o_w[slot])
        w("sync", vE)
        dvg = dma("sync", dS, Fv(o_wst, 1024, 0, 16), wd["Wg"])
        w("vector", dS, dvg)
        cvg = stamp("vector", vE, lambda e: e.tensor_copy(B(o_sqn, 1024, 0, 16), Fv(o_wst, 1024, 0, 16)))
        w("tensor", vE, cvg)
        for i in range(8):
            on("tensor", lambda e, s=B(o_sqn + i * 128, 128, 0, 16), o=A2bf(i * 16, (i + 1) * 16),
               f=(i == 0), l=(i == 7), ид=arena[0:16, o_id:o_id + 16]:
               e.matmul(o, s, ид, is_transpose=True, start=f, stop=l, skip_group_check=True))
        tv = tdrain()
        w("vector", tE, tv)
        wgv = stamp("vector", vE, lambda e: e.tensor_copy(B(o_wg, 128), A2bf(0, 128)))
        w("tensor", vE, wgv)
        for bnm, po, wd_ in (("bsq", o_pl[0], C), ("bsk", o_pl[1], C), ("bsv", o_pl[2], C),
                             ("bv", o_pl[3], C), ("bg", o_plg, H)):
            build_plane(wd[bnm], po, wd_)

        if STOP_PHASE is not None and STOP_PHASE < 1:
            _en[0] = False
        w("vector", vE)
        w("sync", vE)
        w("sync", sE)
        x_dma = [0] * NCH
        for c in range(min(2, NCH)):
            x_dma[c] = dma("sync", dX, Fv(o_xst + (c % 2) * 2048, 1024), x_d[c * 128:(c + 1) * 128, :])
        fk_red = [0] * 8
        fk_dma = [0] * 8
        vdma_prev = [0, 0]
        m_par = 0
        hv_prev = 0
        sev_prev = 0
        ln_done = [0] * NCH
        skv = 0

        def xnt_of(par):
            return o_xnT if par == 0 else o_xnT2

        def round1(wslot, dsto, plane, xnt_o, also_g=False):
            tv = 0
            for kc in range(8):
                for hf in range(2):
                    last = (kc == 7 and hf == 1 and not also_g)
                    f = mml if last else mm
                    tv = f(A(hf), B(xnt_o + kc * 128, 128), B(wslot + kc * 1024 + hf * 512, 512),
                           kc == 0, kc == 7) or tv
                if also_g:
                    f = mml if kc == 7 else mm
                    tv = f(psB[:, 1536:1536 + H], B(xnt_o + kc * 128, 128), B(o_wg + kc * 16, 16),
                           kc == 0, kc == 7) or tv
            w("vector", tE, tv)
            ev = stamp("vector", vE, lambda e, d=B(dsto, 1024), pl=B(plane, 1024):
                       e.tensor_tensor(d, psA[:, 0:1024], pl, op=ALU.add))
            w("tensor", vE, ev)
            return ev

        xnt_tv = [0]
        ln_se = [0]

        def frontA(c):
            bsel = c % 2
            tap_sync("vector")
            tap_sync("scalar")
            w("vector", dX)
            ln_se[0] = ln_stats(o_xst + bsel * 2048, o_st1, o_lnsq)

            if c < 8:
                if c >= 1:
                    w("sync", vE, fk_red[c - 1])
                else:
                    w("sync", vE)
                fk_dma[c] = dma("sync", dF, Fv(o_wst + bsel * 2048, 1024), wd["fk"][c * 128:(c + 1) * 128, :])
                w("vector", dF)
                w("vector", vE)
                stamp("vector", vE, lambda e, i=Fv(o_wst + bsel * 2048, 1024), o=Fv(o_st2 + 96, 1):
                      e.tensor_reduce(o, i, axis=AX.X, op=ALU.max, apply_absolute_value=True))
                vsync()
                fk_red[c] = stamp("vector", vE, lambda e: e.tensor_tensor(
                    Fv(o_kmax, 1), Fv(o_kmax, 1), Fv(o_st2 + 96, 1), op=ALU.max))

        def frontB(c):
            nonlocal sev_prev
            bsel = c % 2
            par = c % 2
            w("vector", tE, xnt_tv[0])
            lnv = ln_norm(o_xst + bsel * 2048, o_xnn, o_st1, ln_se[0])
            ln_done[c] = lnv
            tap_store("xn", B(o_xnn, 1024), c * 128)
            if c + 2 < NCH:
                w("sync", vE, lnv)
                x_dma[c + 2] = dma("sync", dX, Fv(o_xst + (c % 2) * 2048, 1024),
                                   x_d[(c + 2) * 128:(c + 3) * 128, :])
            w("tensor", vE, lnv)
            tg, _ = transpose_group([B(o_xnn + i * 128, 128) for i in range(8)], B(xnt_of(par), 1024))
            xnt_tv[0] = tg
            w("tensor", sE, sev_prev)
            round1(o_w[0], o_sqn, o_pl[0], xnt_of(par), also_g=True)
            tap_store("sq", B(o_sqn, 1024), c * 128)
            gv = stamp("vector", vE, lambda e: e.tensor_tensor(
                Fv(o_st2, H), psB[:, 1536:1536 + H], B(o_plg, H), op=ALU.add))
            w("scalar", vE, gv)
            ge1 = stamp("scalar", sE, lambda e: e.activation(Fv(o_st2 + 32, H), Fv(o_st2, H), AF.Exp))
            w("scalar", sE, ge1)
            spv = stamp("scalar", sE, lambda e: e.activation(Fv(o_sp, H), Fv(o_st2 + 32, H), AF.Ln, bias=cst(1.0)))
            tap_store("sp", Fv(o_sp, H), c * 128)

            w("vector", dG)
            round1(o_w[3], o_vst + bsel * 1024, o_pl[3], xnt_of(par))
            tap_store("v", B(o_vst + bsel * 1024, 1024), c * 128)
            w("gpsimd", vE)
            vdma_prev[bsel] = dma("gpsimd", dG, vext_s[:, c * 1024:(c + 1) * 1024],
                                  B(o_vst + bsel * 1024, 1024))

            transpose_group([B(o_sqn + i * 128, 128) for i in range(8)], B(o_sqT + par * 1024, 1024))

            w("scalar", vE, hv_prev)
            w("tensor", sE, spv)
            mm(psB[:, 1552:1552 + H], Fv(o_btri, 128), Fv(o_sp, H), True, True)
            mm(psB[:, 1568:1568 + H], Fv(o_bones, 128), Fv(o_sp, H), True, True)
            ltv = mml(psB[:, 1584:1584 + H], Fv(o_bones + 256, 128), Fv(o_sp, H), True, True)
            w("scalar", tE, ltv)
            on("scalar", lambda e, o=Fv(o_eml + par * 32, H): e.activation(o, psB[:, 1552:1552 + H], AF.Exp))
            on("scalar", lambda e, o=Fv(o_el + par * 32, H): e.activation(o, psB[:, 1552:1552 + H], AF.Exp, scale=cst(-1.0)))
            on("scalar", lambda e: e.activation(Fv(o_ea, H), psB[:, 1568:1568 + H], AF.Exp, scale=cst(-1.0)))
            sev = stamp("scalar", sE, lambda e: e.activation(Fv(o_ea + 32, H), psB[:, 1584:1584 + H],
                                                             AF.Exp, scale=cst(-1.0)))
            sev_prev = sev
            w("vector", sE, sev)
            for sub in range(2):
                src = o_ea + sub * 32
                even = Fv(src, 16, 0, 64).rearrange("p (a b) -> p a b", b=2)[:, :, 0:1].rearrange("p a b -> p (a b)")
                odd = Fv(src, 16, 64, 128).rearrange("p (a b) -> p a b", b=2)[:, :, 1:2].rearrange("p a b -> p (a b)")
                stamp("vector", vE, lambda e, s=even, d=Fv(o_ep + par * 32 + sub * 16, 8, 0, 64): e.tensor_copy(d, s))
                stamp("vector", vE, lambda e, s=odd, d=Fv(o_ep + par * 32 + sub * 16, 8, 64, 128): e.tensor_copy(d, s))

        def back(c):
            nonlocal m_par, hv_prev, skv
            par = c % 2
            tap_sync("vector")
            round1(o_w[2], o_svn, o_pl[2], xnt_of(par))
            tap_store("sv", B(o_svn, 1024), c * 128)
            round1(o_w[1], o_skn, o_pl[1], xnt_of(par))
            vsync()
            skv = stamp("vector", vE, lambda e, eb=bcast_h(o_eml + par * 32): e.tensor_tensor(
                g3(B(o_sk2, 1024)), g3(B(o_skn, 1024)), eb, op=ALU.mult))
            tap_store("sk2", B(o_sk2, 1024), c * 128)
            w("tensor", vE, skv)
            transpose_group([B(o_sk2 + i * 128, 128) for i in range(8)], B(o_sk2T, 1024))

            w("tensor", vE, hv_prev)
            mv = 0
            for sub in range(2):
                r0 = sub * 64
                for hp_g in range(2):
                    for h in range(hp_g, H, 2):
                        j, hp = h // 2, h % 2
                        bank, ch = (0, h) if h < 8 else (1, h - 8)
                        mm(Bk(bank, r0, r0 + 64, ch * 64, ch * 64 + 64),
                           B(o_sqT + par * 1024 + j * 128 + r0, 64, hp * 64, hp * 64 + 64),
                           B(o_m + m_par * 512 + j * 64, 64, hp * 64, hp * 64 + 64),
                           h in (0, 8), False, tile=(hp * 64, r0))
                    on("tensor", lambda e: e.drain())
                for half in range(2):
                    tv = 0
                    for hp_g in range(2):
                        for hh in range(hp_g, 8, 2):
                            h = half * 8 + hh
                            j, hp = h // 2, h % 2
                            f = mml if hh == 7 else mm
                            tv = f(A(3, r0, r0 + 64, hh * 64, hh * 64 + 64),
                                   B(o_sk2T + j * 128 + r0, 64, hp * 64, hp * 64 + 64),
                                   B(o_sqT + par * 1024 + j * 128 + r0, 64, hp * 64, hp * 64 + 64),
                                   hh == 0, hh == 7, tile=(hp * 64, r0)) or tv
                        on("tensor", lambda e: e.drain())
                    w("vector", tE, tv)
                    mk = arena[r0:r0 + 64, o_mask:o_mask + 64].unsqueeze(1).broadcast_to([64, 8, 64])
                    pv = stamp("vector", vE, lambda e, o=g3(B(o_pp + half * 512, 512, r0, r0 + 64), 8),
                               i=g3(A(3, r0, r0 + 64), 8), m=mk: e.tensor_tensor(o, i, m, op=ALU.mult))
                    w("tensor", vE, pv)
                for h in range(H):
                    half, hh = h // 8, h % 8
                    bank, ch = (0, h) if h < 8 else (1, h - 8)
                    mm(Bk(bank, r0, r0 + 64, ch * 64, ch * 64 + 64),
                       B(o_pp + half * 512 + hh * 64, 64, r0, r0 + 64),
                       B(o_svn + h * 64, 64, r0, r0 + 64),
                       False, h in (7, 15), tile=(r0, r0))
                stv = 0
                for h in range(H):
                    j, hp = h // 2, h % 2
                    f = mml if h == 15 else mm
                    stv = f(Bk(2, hp * 64, hp * 64 + 64, j * 64, j * 64 + 64),
                            B(o_sk2 + h * 64, 64, r0, r0 + 64),
                            B(o_svn + h * 64, 64, r0, r0 + 64),
                            h in (0, 1), h in (14, 15), tile=(r0, hp * 64)) or stv
                w("vector", tE, stv)
                w("vector", vE)
                stamp("vector", vE, lambda e, mp=m_par: e.tensor_tensor(
                    Fv(o_pp, 512), Bk(2), B(o_m + mp * 512, 512), op=ALU.add))
                vsync()
                ep_b = Fv(o_ep + par * 32 + sub * 16, 8).unsqueeze(2).broadcast_to([128, 8, 64])
                mv = stamp("vector", vE, lambda e, np_=1 - m_par, eb=ep_b: e.tensor_tensor(
                    g3(B(o_m + np_ * 512, 512), 8), g3(Fv(o_pp, 512), 8), eb, op=ALU.mult))
                w("tensor", vE, mv)
                m_par = 1 - m_par
            hv_prev = stamp("vector", vE, lambda e, eb=bcast_h(o_el + par * 32): e.tensor_tensor(
                g3(B(o_hn, 1024)), g3(psB[:, 0:1024]), eb, op=ALU.mult))
            tap_store("hseq", B(o_hn, 1024), c * 128)
            w("tensor", vE, hv_prev)
            transpose_group([B(o_hn + i * 128, 128) for i in range(8)], B(o_big + c * 1024, 1024))

        for c in range(NCH + 1):
            if c < NCH:
                frontA(c)
            if c >= 1:
                back(c - 1)
            if c < NCH:
                frontB(c)

        if STOP_PHASE is not None and STOP_PHASE < 2:
            _en[0] = False
        prep_weightT(wd["Wq"], o_w2[0])
        prep_weightT(wd["Wk"], o_w2[1])
        build_plane(wd["bq"], o_pl[0], C)
        build_plane(wd["bk"], o_pl[1], C)
        w("gpsimd", vE)
        with nc.allow_non_contiguous_dma("fold [128,1] maxabs to one row"):
            gmx = dma("gpsimd", dG, Fv(o_small + 128, 128, 0, 1), Fv(o_kmax, 1))
        w("vector", dG)
        rmx = stamp("vector", vE, lambda e: e.tensor_reduce(
            Fv(o_kmax + 2, 1, 0, 1), Fv(o_small + 128, 128, 0, 1), axis=AX.X, op=ALU.max))
        w("tensor", vE, rmx)
        mm(psA[:, 1536:1537], Fv(o_onef, 128, 0, 1), Fv(o_kmax + 2, 1, 0, 1), True, True)
        tvm = tdrain()
        w("vector", tE, tvm)
        stamp("vector", vE, lambda e: e.tensor_copy(Fv(o_kmax + 8, 1), psA[:, 1536:1537]))
        vsync()
        stamp("vector", vE, lambda e: e.reciprocal(Fv(o_kmax + 12, 1), Fv(o_kmax + 8, 1)))
        vsync()
        stamp("vector", vE, lambda e: e.tensor_scalar(Fv(o_kmax + 12, 1), Fv(o_kmax + 12, 1), 7.0, None, op0=ALU.mult))
        vsync()
        fsc = stamp("vector", vE, lambda e: e.tensor_scalar(Fv(o_kmax + 8, 1), Fv(o_kmax + 8, 1),
                                                            1.0 / 7.0, None, op0=ALU.mult))

        qdma_prev = 0
        vload = [0] * NCH
        rnd = [0] * NCH
        krnd = [0] * NCH

        def norm_path(pfull, plane, rawo, rso, outo, fold_cb, tgate):
            tap_sync("vector")
            first = stamp("vector", vE, lambda e, pf=pfull, pl=B(plane, 1024):
                          e.tensor_tensor(B(rawo, 1024), pf, pl, op=ALU.add))
            vsync()
            stamp("vector", vE, lambda e: e.tensor_tensor(Fv(o_sk2, 1024), B(rawo, 1024), B(rawo, 1024), op=ALU.mult))
            vsync()
            rv = stamp("vector", vE, lambda e: e.tensor_reduce(
                Fv(rso, H), Fv(o_sk2, 1024).rearrange("p (a b) -> p a b", a=H), axis=AX.X, op=ALU.add))
            w("scalar", vE, rv)
            nl = stamp("scalar", sE, lambda e: e.activation(Fv(rso + 32, H), Fv(rso, H), AF.Ln, bias=cst(1e-19)))
            w("scalar", sE, nl)
            sv2 = stamp("scalar", sE, lambda e: e.activation(Fv(rso, H), Fv(rso + 32, H), AF.Exp, scale=cst(-0.5)))
            w("vector", sE, sv2)
            if fold_cb:
                stamp("vector", vE, lambda e: e.tensor_tensor(Fv(rso, H), Fv(rso, H), Fv(o_cb, H), op=ALU.mult))
                vsync()
            w("vector", tE, tgate)
            nv = stamp("vector", vE, lambda e: e.tensor_tensor(
                g3(B(outo, 1024)), g3(B(rawo, 1024)), bcast_h(rso), op=ALU.mult))
            return first, nv

        frdK = frdQ = 0
        nk = nq = 0
        tkv = tqv2 = 0
        for c in range(NCH + 1):
            if c >= 1:
                c1 = c - 1
                w("vector", tE, krnd[c1])
                frdK, nk = norm_path(psA[:, 0:1024], o_pl[1], o_sqn, o_rstd, o_hn, True, tkv)
                w("vector", tE, rnd[c1])
                frdQ, nq = norm_path(psB[:, 0:1024], o_pl[0], o_skn, o_rstd + 64, o_hn2, False, tqv2)
            if c < NCH:
                w("tensor", vE, frdK)
                kv = 0
                for kc in range(8):
                    for hf in range(2):
                        f = mml if (kc == 7 and hf == 1) else mm
                        kv = f(A(hf), B(o_big + c * 1024 + kc * 128, 128),
                               B(o_w[1] + kc * 1024 + hf * 512, 512), kc == 0, kc == 7) or kv
                krnd[c] = kv
                w("tensor", vE, frdQ)
                tv = 0
                for kc in range(8):
                    for hf in range(2):
                        last = (kc == 7 and hf == 1)
                        f = mml if last else mm
                        tv = f(Bk(hf), B(o_big + c * 1024 + kc * 128, 128),
                               B(o_w[0] + kc * 1024 + hf * 512, 512), kc == 0, kc == 7) or tv
                rnd[c] = tv
            if c >= 1:
                c1 = c - 1
                w("tensor", vE, nk)
                tap_store("kc", B(o_hn, 1024), c1 * 128)
                tkv, _ = transpose_group([B(o_hn + i * 128, 128) for i in range(8)], B(o_kt + c1 * 1024, 1024))
                w("tensor", vE, nq)
                tap_store("qn", B(o_hn2, 1024), c1 * 128)
                w("vector", dG, qdma_prev)
                tqv2, _ = transpose_group([B(o_hn2 + i * 128, 128) for i in range(8)], B(o_sqT, 1024))
                tb_, tof = c1 // 4, (c1 % 4) * 128
                qdst = qt_s[:, tb_ * 4096:(tb_ + 1) * 4096].rearrange("p (a b) -> p a b", b=512)[:, :, tof:tof + 128]
                w("gpsimd", vE)
                qdma_prev = dma("gpsimd", dG, qdst, B(o_sqT, 1024).rearrange("p (a b) -> p a b", b=128))
                w("sync", tE, rnd[c1])
                w("sync", dG)
                vload[c1] = dma("sync", dS, arena[:, o_big + c1 * 1024:o_big + (c1 + 1) * 1024],
                                vext_s[:, c1 * 1024:(c1 + 1) * 1024])

        if STOP_PHASE is not None and STOP_PHASE < 3:
            _en[0] = False
        prep_weightT(wd["Wo"], o_w2[0], stg=o_p3, stg2=o_rv)
        build_plane(wd["bo"], o_pl3, C, brow=o_p3)
        for eng in Prog.ENGINES:
            w(eng, tE)
            w(eng, vE)
            w(eng, sE)
            w(eng, dG)
            w(eng, dS)
            w(eng, dX)
            w(eng, dF)
            w(eng, dT)
        dma("sync", dS, Fv(o_ind, 256), ind_d)
        w("tensor", dS)
        if "vx" in tap_d:
            dma("gpsimd", dT, tap_d["vx"], B(o_big, T * 8))
        on("vector", lambda e: e.memset(Fv(o_rv, 1024), 0.0))
        rvz = stamp("vector", vE, lambda e: e.memset(B(o_aot, 4096), 0.0))
        w("tensor", vE, rvz)

        qld = [0] * NTB
        qsem = [dX, dF]
        for tb in range(min(2, NTB)):
            qld[tb] = dma("sync", qsem[tb % 2], B(o_qtb + (tb % 2) * 4096, 4096), qt_s[:, tb * 4096:(tb + 1) * 4096])
        prev_exp_par = [0, 0]
        pp_attnv = [0, 0, 0, 0]
        adma_prev = [0, 0]
        st3 = {"aot_par": [0, 0], "rec_par": [0, 0], "bk3": 0, "aot_last": 0}

        def emit_wo_piece(tb, piece):
            t1, hf = piece >> 1, piece & 1
            w("tensor", vE, st3["bk3"])
            if piece == 0:
                w("tensor", vE, st3["aot_last"])
            wov = 0
            for j in range(8):
                f = mml if j == 7 else mm
                wov = f(Bk(3), B(o_aot + j * 512 + t1 * 128, 128),
                        B(o_w2[0] + j * 1024 + hf * 512, 512), j == 0, j == 7) or wov
            w("vector", tE, wov)
            if hf == 0:
                w("vector", dG, adma_prev[t1 % 2])
            ev = stamp("vector", vE, lambda e, o=B(o_att + (t1 % 2) * 1024 + hf * 512, 512),
                       i=Bk(3), pl=B(o_pl3 + hf * 512, 512): e.tensor_tensor(o, i, pl, op=ALU.add))
            st3["bk3"] = ev
            if hf == 1:
                tap_sync("vector")
                tap_store("attn", B(o_att + (t1 % 2) * 1024, 1024), tb * 512 + t1 * 128)
                w("gpsimd", vE, ev)
                w("gpsimd", dG)
                adma_prev[t1 % 2] = dma("gpsimd", dG,
                                        attn_s[tb * 512 + t1 * 128:tb * 512 + (t1 + 1) * 128, :],
                                        B(o_att + (t1 % 2) * 1024, 1024))

        def emit_wo(tb):
            for piece in range(8):
                emit_wo_piece(tb, piece)

        def emit_epi(g, av_last):
            par = g & 1
            abk = 0 if par == 0 else 2
            pr = g & 7
            w("tensor", vE, st3["rec_par"][par])
            w("tensor", vE, st3["bk3"])
            bc = mml(Bk(3), Fv(o_ind + par * 256, 128), Fv(o_rv + par * 1024, 512), True, True)
            w("vector", tE, bc)
            w("vector", vE, st3["aot_last"])
            tap_sync("vector")
            bcv = stamp("vector", vE, lambda e, o=Fv(o_bcv, 512), i=Bk(3): e.tensor_copy(o, i))
            st3["bk3"] = bcv
            w("vector", vE, bcv)
            aot_ev = stamp("vector", vE, lambda e, o=B(o_aot + pr * 512, 512),
                           i0=Bk(abk), i1=Fv(o_bcv, 512): e.tensor_tensor(o, i0, i1, op=ALU.mult))
            st3["aot_par"][par] = aot_ev
            st3["aot_last"] = aot_ev

        av_prev = 0
        NG = NTB * 8
        hoist_scv = [0, 0]

        def emit_scores(g2, s2):
            tb2, pr2 = g2 >> 3, g2 & 7
            qb2 = (tb2 % 2) * 4096
            sb2 = s2 % 2
            w("tensor", sE, prev_exp_par[sb2])
            scv = 0
            for hp in range(2):
                f = mml if hp == 1 else mm
                scv = f(psA[:, sb2 * 1024 + hp * 512:sb2 * 1024 + hp * 512 + 512],
                        B(o_kt + s2 * 1024 + pr2 * 128, 128, hp * 64, hp * 64 + 64),
                        B(o_qtb + qb2 + pr2 * 512, 512, hp * 64, hp * 64 + 64),
                        True, True, tile=(hp * 64, 0)) or scv
            return scv

        for g in range(NG):
            tb, pr = g >> 3, g & 7
            par = g & 1
            abk = 0 if par == 0 else 2
            rbase = par * 32
            if pr == 0 and g == 0:
                w("tensor", qsem[tb % 2], qld[tb])
            tap_sync("scalar")
            tap_sync("vector")
            bk0_gate = st3["aot_par"][par]
            bk1_gate = st3["rec_par"][par]
            exp_hist = [0, 0, 0, 0]
            for s in range(NST + 2):
                if s < NST:
                    sb = s % 2
                    if g > 0 and s < 2:
                        scv = hoist_scv[s]
                    else:
                        scv = emit_scores(g, s)
                    w("scalar", tE, scv)
                    w("scalar", tE, pp_attnv[s % 4])
                    exp_s = stamp("scalar", sE, lambda e, o=B(o_p3 + (s % 4) * 1024, 1024),
                                  i=psA[:, sb * 1024:sb * 1024 + 1024]: e.activation(o, i, AF.Exp))
                    prev_exp_par[sb] = exp_s
                    exp_hist[s % 4] = exp_s
                if s >= 2:
                    s1 = s - 2
                    sb1 = s1 % 4
                    w("tensor", sE, exp_hist[sb1])
                    if s1 == 0:
                        w("tensor", vE, bk0_gate)
                    for hp in range(2):
                        h = pr * 2 + hp
                        mm(Bk(abk, hp * 64, hp * 64 + 64),
                           arena[:, o_big + s1 * 1024 + h * 64:o_big + s1 * 1024 + h * 64 + 64],
                           B(o_p3 + sb1 * 1024 + hp * 512, 512),
                           s1 == 0, s1 == NST - 1, tile=(0, hp * 64))
                    if s1 == 0:
                        w("tensor", vE, bk1_gate)
                    for hp in range(2):
                        f = mml if hp == 1 else mm
                        pp_attnv[sb1] = f(Bk(1, rbase + hp * 64, rbase + hp * 64 + 1),
                                          arena[:, o_onec:o_onec + 1],
                                          B(o_p3 + sb1 * 1024 + hp * 512, 512),
                                          s1 == 0, s1 == NST - 1,
                                          tile=(0, rbase + hp * 64)) or pp_attnv[sb1]
                if pr == 1 and tb >= 1 and 0 < s <= NST and (s * 8) % NST == 0:
                    emit_wo_piece(tb - 1, s * 8 // NST - 1)
                if s == max(2, (5 * NST) // 8) and g >= 1 and not (pr == 1 and tb >= 1):
                    emit_epi(g - 1, av_prev)
                if s == NST - 1 and g + 1 < NG:
                    if (g + 1) & 7 == 0:
                        w("tensor", qsem[((g + 1) >> 3) % 2], qld[(g + 1) >> 3])
                    for s2 in (0, 1):
                        hoist_scv[s2] = emit_scores(g + 1, s2)
            av_last = pp_attnv[(NST - 1) % 4]
            if g >= 1 and pr == 1 and tb >= 1:
                emit_epi(g - 1, av_prev)
            w("vector", tE, av_last)
            rc = 0
            for rr in (rbase, rbase + 64):
                rc = stamp("vector", vE, lambda e, i=Bk(1, rr, rr + 1),
                           o=Fv(o_rv + par * 1024, 512, rr, rr + 1): e.reciprocal(o, i))
            if pr == 3 and tb + 2 < NTB:
                w("sync", tE)
                qld[tb + 2] = dma("sync", qsem[tb % 2], B(o_qtb + (tb % 2) * 4096, 4096),
                                  qt_s[:, (tb + 2) * 4096:(tb + 3) * 4096])
            av_prev = av_last
            st3["rec_par"][par] = rc
        emit_epi(NTB * 8 - 1, av_prev)
        emit_wo(NTB - 1)

        if STOP_PHASE is not None and STOP_PHASE < 4:
            _en[0] = False
        for eng in Prog.ENGINES:
            w(eng, tE)
            w(eng, vE)
            w(eng, sE)
            w(eng, dG)
            w(eng, dS)
            w(eng, dX)
            w(eng, dF)
            w(eng, dT)
        dva = dma("sync", dS, Fv(o_fst, 128).rearrange("p (a b) -> p a b", b=16),
                  wd["fa"].rearrange("(a p) r -> p a r", p=128))
        dvb = dma("sync", dS, Fv(o_fst + 2048, 1024, 0, 16), wd["fb"])
        w("vector", dS, dvb)
        on("vector", lambda e: e.tensor_copy(B(o_xn2, 128), Fv(o_fst, 128)))
        cvb = stamp("vector", vE, lambda e: e.tensor_copy(B(o_fbB, 1024, 0, 16), Fv(o_fst + 2048, 1024, 0, 16)))
        w("tensor", vE, cvb)
        for i in range(8):
            on("tensor", lambda e, s=B(o_xn2 + i * 16, 16), o=A2bf(i * 128, (i + 1) * 128, 0, 16),
               f=(i == 0), l=(i == 7), ид=arena[:, o_id:o_id + 128]:
               e.matmul(o, s, ид, is_transpose=True, start=f, stop=l, skip_group_check=True))
        tva = tdrain()
        w("vector", tE, tva)
        ave = stamp("vector", vE, lambda e: e.tensor_copy(B(o_aT, 1024, 0, 16), A2bf(0, 1024, 0, 16)))
        w("tensor", vE, ave)
        build_plane(wd["fbias"], o_pl4, C, brow=o_x4a)
        for kc in range(8):
            bsel = kc % 2
            w("sync", vE)
            dv = dma("sync", dS, Fv(o_fst + bsel * 2048, 1024), wd["fk"][kc * 128:(kc + 1) * 128, :])
            w("vector", dS, dv)
            w("vector", vE, fsc)
            w("vector", vE)
            stamp("vector", vE, lambda e, i=Fv(o_fst + bsel * 2048, 1024): e.tensor_scalar(
                Fv(o_wq4, 1024), i, Fv(o_kmax + 12, 1), MAGIC, op0=ALU.mult, op1=ALU.add))
            vsync()
            stamp("vector", vE, lambda e: e.tensor_scalar(
                Fv(o_wq4, 1024), Fv(o_wq4, 1024), -MAGIC, Fv(o_kmax + 8, 1), op0=ALU.add, op1=ALU.mult))
            tv = 0
            for hf in range(2):
                f = mml if hf == 1 else mm
                tv = f(A(hf), B(o_aT + kc * 128, 128, 0, 16), B(o_fbB + hf * 512, 512, 0, 16), True, True) or tv
            w("vector", tE, tv)
            vsync()
            wfv = stamp("vector", vE, lambda e, d=B(o_w2[1] + kc * 1024, 1024):
                        e.tensor_tensor(d, psA[:, 0:1024], Fv(o_wq4, 1024), op=ALU.add))
            w("tensor", vE, wfv)
        x4a = [0] * NCH
        x4b = [0] * NCH
        ln4 = [0] * NCH
        ln4se = [0] * NCH
        w("sync", tE)
        w("sync", vE)
        for c in range(min(2, NCH)):
            x4a[c] = dma("sync", dX, Fv(o_x4a + (c % 2) * 2048, 1024), x_d[c * 128:(c + 1) * 128, :])
            x4b[c] = dma("sync", dF, B(o_x4b + (c % 2) * 1024, 1024), attn_s[c * 128:(c + 1) * 128, :])
        odma_prev = [0, 0]
        out_dma = 0
        yfree = 0
        xnT_gate = 0
        w("vector", tE, tva)
        for c in range(NCH + 1):
            if c < NCH:
                bsel = c % 2
                tap_sync("vector")
                tap_sync("sync")
                w("vector", dX)
                w("vector", dF)
                x1v = stamp("vector", vE, lambda e, a=Fv(o_x4a + bsel * 2048, 1024),
                            b=B(o_x4b + bsel * 1024, 1024): e.tensor_tensor(a, a, b, op=ALU.add))
                tap_store("x1", Fv(o_x4a + bsel * 2048, 1024), c * 128)
                vsync()
                ln4se[c] = ln_stats(o_x4a + bsel * 2048, o_s4 + bsel * 32, o_lnsq4)
            if c >= 1:
                c1 = c - 1
                b1 = c1 % 2
                ln4[c1] = ln_norm(o_x4a + b1 * 2048, o_xn2 + b1 * 1024, o_s4 + b1 * 32, ln4se[c1])
                w("tensor", vE, ln4[c1])
                w("vector", tE, xnT_gate)
                transpose_group([B(o_xn2 + b1 * 1024 + i * 128, 128) for i in range(8)], B(o_x2T, 1024))
                w("tensor", vE, yfree)
                ytv = 0
                for kc in range(8):
                    for hf in range(2):
                        last = (kc == 7 and hf == 1)
                        f = mml if last else mm
                        ytv = f(A(hf), B(o_x2T + kc * 128, 128), B(o_w2[1] + kc * 1024 + hf * 512, 512),
                                kc == 0, kc == 7) or ytv
                xnT_gate = ytv
                w("vector", tE, ytv)
                w("vector", dG, odma_prev[b1])
                yfree = stamp("vector", vE, lambda e, o=Fv(o_ou + b1 * 2048, 1024): e.tensor_tensor(
                    o, psA[:, 0:1024], B(o_pl4, 1024), op=ALU.add))
                vsync()
                ov_ = stamp("vector", vE, lambda e, o=Fv(o_ou + b1 * 2048, 1024),
                            x1=Fv(o_x4a + b1 * 2048, 1024): e.tensor_tensor(o, o, x1, op=ALU.add))
                w("gpsimd", vE, ov_)
                w("gpsimd", dG)
                out_dma = dma("gpsimd", dG, out_d[c1 * 128:(c1 + 1) * 128, :], Fv(o_ou + b1 * 2048, 1024))
                odma_prev[b1] = out_dma
                if c1 + 2 < NCH:
                    tap_sync("sync")
                    w("sync", vE, ov_)
                    x4a[c1 + 2] = dma("sync", dX, Fv(o_x4a + b1 * 2048, 1024),
                                      x_d[(c1 + 2) * 128:(c1 + 3) * 128, :])
                    x4b[c1 + 2] = dma("sync", dF, B(o_x4b + b1 * 1024, 1024),
                                      attn_s[(c1 + 2) * 128:(c1 + 3) * 128, :])
        w("gpsimd", dG, out_dma)

        blk = ctx.enter_context(nc.Block())
        p.run(blk)
    return nc


def make_consts():
    ident = np.eye(128, dtype=ml_dtypes.bfloat16)
    u = np.arange(128)
    btri = ((u[:, None] // 64 == u[None, :] // 64) & (u[:, None] <= u[None, :])).astype(np.float32)
    bones = np.zeros((128, 256), np.float32)
    bones[0:64, 0:128] = 1.0
    bones[64:128, 128:256] = 1.0
    m = (np.arange(64)[:, None] <= np.arange(64)[None, :]).astype(np.float32)
    mask2 = np.concatenate([m, m], 0).astype(ml_dtypes.bfloat16)
    ind = np.zeros((128, 256), np.float32)
    ind[0, 0:64] = 1.0
    ind[64, 64:128] = 1.0
    ind[32, 128:192] = 1.0
    ind[96, 192:256] = 1.0
    identf = np.eye(128, dtype=np.float32)
    return {"ident": ident, "btri": btri, "bones": bones, "mask2": mask2, "ind": ind,
            "identf": identf}


def make_in_map(xb, inputs):
    m = {"x": np.ascontiguousarray(np.asarray(xb, np.float32))}
    ren = {"Wsq": "L0_attn_Wsq", "bsq": "L0_attn_bsq", "Wsk": "L0_attn_Wsk", "bsk": "L0_attn_bsk",
           "Wsv": "L0_attn_Wsv", "bsv": "L0_attn_bsv", "Wg": "L0_attn_Wg", "bg": "L0_attn_bg",
           "Wq": "L0_attn_Wq", "bq": "L0_attn_bq", "Wk": "L0_attn_Wk", "bk": "L0_attn_bk",
           "Wv": "L0_attn_Wv", "bv": "L0_attn_bv", "Wo": "L0_attn_Wo", "bo": "L0_attn_bo",
           "temp": "L0_attn_temp", "fk": "L0_ffn_kernel", "fbias": "L0_ffn_bias",
           "fa": "L0_ffn_a", "fb": "L0_ffn_b"}
    for k, src in ren.items():
        v = np.asarray(inputs[src], np.float32)
        if k in ("bsq", "bsk", "bsv", "bq", "bk", "bv", "bo", "fbias", "bg", "temp"):
            v = v.reshape(1, -1)
        m[k] = np.ascontiguousarray(v)
    m.update(make_consts())
    return m


def kernel(**inputs):
    x = np.asarray(inputs["L0_attn_x"], np.float32)
    Bn, T, _ = x.shape
    nc = build(T=T)
    in_maps = [make_in_map(x[b], inputs) for b in range(Bn)]
    res = run_bass_kernel_spmd(nc, in_maps, core_ids=list(range(Bn)))
    return np.stack([np.asarray(r["out"], np.float32) for r in res.results], 0)
